# version 14
# baseline (speedup 1.0000x reference)
import contextlib
import numpy as np
import ml_dtypes
import concourse.bass as bass
import concourse.mybir as mybir
from concourse.bass_utils import run_bass_kernel_spmd

F32 = mybir.dt.float32
BF16 = mybir.dt.bfloat16
I32 = mybir.dt.int32
AF = mybir.ActivationFunctionType
ALU = mybir.AluOpType
AX = mybir.AxisListType

NEG = -1.0e30


class Cfg:
    def __init__(self, H=4096, I=11008, NHQ=32, NHKV=8, B=4, S=4096,
                 n_cores=8, theta=10000.0, eps=1e-5):
        self.H, self.I, self.NHQ, self.NHKV = H, I, NHQ, NHKV
        self.B, self.S, self.n_cores = B, S, n_cores
        self.theta, self.eps = theta, eps
        self.D = 128
        assert NHQ * self.D == H
        self.KVD = NHKV * self.D
        self.HT = H // 128
        self.IT = I // 128
        self.KVT = self.KVD // 128
        self.CH = 512
        self.CT = self.CH // 128
        self.SC = S // self.CH
        assert n_cores == 2 * B
        self.NC = self.SC // 2
        self.GRP = NHQ // NHKV
        self.posX = [p for p in range(self.SC) if p % 4 in (0, 3)]
        self.posY = [p for p in range(self.SC) if p % 4 in (1, 2)]
        assert len(self.posX) == self.NC and len(self.posY) == self.NC
        self.slot_blocks = [max(self.posX[s], self.posY[s]) + 1
                            for s in range(self.NC)]


class Buf:
    __slots__ = ("name", "writers", "readers", "view")

    def __init__(self, name, view=None):
        self.name = name
        self.writers = []
        self.readers = []
        self.view = view


class Sem:
    __slots__ = ("h", "n", "name")

    def __init__(self, h, name):
        self.h = h
        self.n = 0
        self.name = name


class DStream:

    def __init__(self, k, name, nslots=2):
        self.sems = [k.mk_sem(f"{name}{i}") for i in range(nslots)]


class K:
    ENGS = {"t": "tensor", "v": "vector", "a": "scalar", "s": "sync",
            "g": "gpsimd"}

    def __init__(self, nc, ex, blk):
        self.nc = nc
        self.ex = ex
        self.blk = blk
        self._semnames = 0
        self.esem = {e: self.mk_sem(f"e_{e}") for e in ("t", "v", "a", "g")}
        self.wm = {e: {} for e in self.ENGS}

    def mk_sem(self, name):
        h = self.ex.enter_context(self.nc.semaphore(name))
        return Sem(h, name)

    def _wait(self, eng, ename, tok):
        sem, val = tok
        if val <= 0:
            return
        if self.wm[ename].get(id(sem), -1) >= val:
            return
        eng.wait_ge(sem.h, val)
        self.wm[ename][id(sem)] = val

    def wait_reads(self, eng, ename, bufs):
        for b in bufs:
            for tok in b.writers:
                self._wait(eng, ename, tok)

    def wait_writes(self, eng, ename, bufs):
        for b in bufs:
            for tok in b.writers:
                self._wait(eng, ename, tok)
            for tok in b.readers:
                self._wait(eng, ename, tok)

    def note(self, ename, tok, reads=(), writes=()):
        for b in reads:
            b.readers.append(tok)
        for b in writes:
            b.writers = [tok]
            b.readers = []

    def on(self, ename, body, reads=(), writes=()):
        sem = self.esem[ename]
        out = {}

        def _f(eng):
            self.wait_reads(eng, ename, reads)
            self.wait_writes(eng, ename, writes)
            last = body(eng)
            assert last is not None, "step body must return last instruction"
            last.then_inc(sem.h, 1)
            sem.n += 1
            out["tok"] = (sem, sem.n)

        getattr(self.blk, self.ENGS[ename])(_f)
        tok = out["tok"]
        self.note(ename, tok, reads, writes)
        return tok

    def dma(self, qname, sem, transfers, reads=(), writes=()):
        out = {}

        def _f(eng):
            self.wait_reads(eng, qname, reads)
            self.wait_writes(eng, qname, writes)
            for tr in transfers:
                dst, src = tr[0], tr[1]
                kw = tr[2] if len(tr) > 2 else {}
                eng.dma_start(out=dst, in_=src, **kw).then_inc(sem.h, 16)
                sem.n += 16
            out["n"] = sem.n

        getattr(self.blk, self.ENGS[qname])(_f)
        tok = (sem, out["n"])
        self.note(qname, tok, reads, writes)
        return tok

    def raw(self, ename, body):
        getattr(self.blk, self.ENGS[ename])(lambda eng: body(eng))

    def batch(self, ename, items):
        sem = self.esem[ename]
        toks = []

        def _f(eng):
            for body, reads, writes in items:
                self.wait_reads(eng, ename, reads)
                self.wait_writes(eng, ename, writes)
                last = body(eng)
                last.then_inc(sem.h, 1)
                sem.n += 1
                tok = (sem, sem.n)
                self.note(ename, tok, reads, writes)
                toks.append(tok)

        getattr(self.blk, self.ENGS[ename])(_f)
        return toks

    def barrier(self, dma_sems=()):
        toks = [(s, s.n) for s in self.esem.values()]
        toks += [(s, s.n) for s in dma_sems]
        for ename in ("t", "v", "a", "s", "g"):
            def _f(eng, ename=ename):
                for tok in toks:
                    self._wait(eng, ename, tok)
            getattr(self.blk, self.ENGS[ename])(_f)


def ceil_div(a, b):
    return -(-a // b)


def _cast_dma(k, sem, dst_ap, src_ap, rows, row_split=2048):
    trs = []
    for r0 in range(0, rows, row_split):
        r1 = min(rows, r0 + row_split)
        trs.append((dst_ap(r0, r1), src_ap(r0, r1)))
    return k.dma("g", sem, trs)


def build_program(cfg: Cfg):
    c = cfg
    nc = bass.Bass("TRN2", target_bir_lowering=False)

    dr = lambda n, sh, dt, kind: nc.dram_tensor(n, sh, dt, kind=kind).ap()
    sb = lambda n, sh, dt: nc.sbuf_tensor(n, sh, dt,
                                          allow_name_mangling=True)
    xfull = dr("xfull", [c.S, c.H], F32, "ExternalInput")
    xq = dr("xq", [c.NC * c.CH, c.H], F32, "ExternalInput")
    xqT = dr("xqT", [c.H, c.NC * c.CH], F32, "ExternalInput")
    wq = dr("wq", [c.H, c.H], F32, "ExternalInput")
    wk = dr("wk", [c.H, c.KVD], F32, "ExternalInput")
    wv = dr("wv", [c.H, c.KVD], F32, "ExternalInput")
    wo = dr("wo", [c.H, c.H], F32, "ExternalInput")
    gwT = dr("gwT", [c.H, c.I], I32, "ExternalInput")
    uwT = dr("uwT", [c.H, c.I], I32, "ExternalInput")
    dwT = dr("dwT", [c.I, c.H], I32, "ExternalInput")
    gs = dr("gs", [128, c.IT], F32, "ExternalInput")
    us = dr("us", [128, c.IT], F32, "ExternalInput")
    ds = dr("ds", [128, c.HT], F32, "ExternalInput")
    cosk = dr("cosk", [64, c.S], F32, "ExternalInput")
    sink = dr("sink", [64, c.S], F32, "ExternalInput")
    nsink = dr("nsink", [64, c.S], F32, "ExternalInput")
    cosq = dr("cosq", [64, c.NC * c.CH], F32, "ExternalInput")
    sinq = dr("sinq", [64, c.NC * c.CH], F32, "ExternalInput")
    nsinq = dr("nsinq", [64, c.NC * c.CH], F32, "ExternalInput")
    maskT = dr("maskT", [c.NC * 8 * 128, 512], F32, "ExternalInput")
    ident = dr("ident", [128, 128], BF16, "ExternalInput")
    outT = dr("outT", [c.H, c.NC * c.CH], F32, "ExternalOutput")

    wq_c = dr("wq_c", [c.HT, 128, c.HT * 128], BF16, "Internal")
    wo_c = dr("wo_c", [c.HT, 128, c.HT * 128], BF16, "Internal")
    wk_c = dr("wk_c", [128, c.HT * c.KVD], BF16, "Internal")
    wv_c = dr("wv_c", [128, c.HT * c.KVD], BF16, "Internal")
    g_c = dr("g_c", [c.IT, 128, c.HT * 128], BF16, "Internal")
    u_c = dr("u_c", [c.IT, 128, c.HT * 128], BF16, "Internal")
    d_c = dr("d_c", [c.HT, 128, c.IT * 128], BF16, "Internal")
    xnT_h = dr("xnT_h", [c.SC, 128, c.HT * c.CH], BF16, "Internal")
    kt_h = dr("kt_h", [c.NHKV, 128, c.S], BF16, "Internal")
    v_h = dr("v_h", [c.NHKV, c.S, 128], BF16, "Internal")
    ffn_h = dr("ffn_h", [c.NC, c.HT, 128, c.CH], F32, "Internal")

    with contextlib.ExitStack() as EX:
        e = EX.enter_context
        blk = e(nc.Block())
        k = K(nc, EX, blk)

        B_wq_c = Buf("wq_c"); B_wo_c = Buf("wo_c")
        B_wk_c = Buf("wk_c"); B_wv_c = Buf("wv_c")
        B_g_c = Buf("g_c"); B_u_c = Buf("u_c"); B_d_c = Buf("d_c")
        B_xnT = [Buf(f"xnT{i}") for i in range(c.SC)]
        B_kt = Buf("kt_h"); B_v = Buf("v_h")
        B_ffn = [Buf(f"ffn{i}") for i in range(c.NC)]
        B_out = Buf("outT")

        s_cast = {n: k.mk_sem(f"cast_{n}") for n in
                  ("kv", "gu", "dn", "wq", "wo")}
        st_x = DStream(k, "sx", 4)
        st_w = DStream(k, "sw", 3)
        st_f = DStream(k, "sf")
        st_kv = DStream(k, "skv")
        st_sp = DStream(k, "ssp")
        st_w2 = DStream(k, "sw2", 3)
        st_out = DStream(k, "sot")
        s_misc = k.mk_sem("smisc")

        _cast_dma(
            k, s_cast["kv"],
            lambda r0, r1: wk_c[:, r0 // 128 * c.KVD:(r1 // 128) * c.KVD]
            .rearrange("p (t d) -> p t d", d=c.KVD),
            lambda r0, r1: wk[r0:r1, :].rearrange("(t p) d -> p t d", p=128),
            c.H)
        _cast_dma(
            k, s_cast["kv"],
            lambda r0, r1: wv_c[:, r0 // 128 * c.KVD:(r1 // 128) * c.KVD]
            .rearrange("p (t d) -> p t d", d=c.KVD),
            lambda r0, r1: wv[r0:r1, :].rearrange("(t p) d -> p t d", p=128),
            c.H)
        k.note("g", (s_cast["kv"], s_cast["kv"].n),
               writes=[B_wk_c, B_wv_c])

        def cache_col_slabs(sem, dst, src, m, H_rows):
            _cast_dma(
                k, sem,
                lambda r0, r1: dst[m, :, r0:r1]
                .rearrange("p (t o) -> p t o", o=128),
                lambda r0, r1: src[r0:r1, m * 128:(m + 1) * 128]
                .rearrange("(t p) o -> p t o", p=128),
                H_rows)

        for m in range(c.IT):
            cache_col_slabs(s_cast["gu"], g_c, gwT, m, c.H)
            cache_col_slabs(s_cast["gu"], u_c, uwT, m, c.H)
        k.note("g", (s_cast["gu"], s_cast["gu"].n), writes=[B_g_c, B_u_c])
        for m in range(c.HT):
            cache_col_slabs(s_cast["dn"], d_c, dwT, m, c.I)
        k.note("g", (s_cast["dn"], s_cast["dn"].n), writes=[B_d_c])
        for m in range(c.HT):
            cache_col_slabs(s_cast["wq"], wq_c, wq, m, c.H)
        k.note("g", (s_cast["wq"], s_cast["wq"].n), writes=[B_wq_c])
        for m in range(c.HT):
            cache_col_slabs(s_cast["wo"], wo_c, wo, m, c.H)
        k.note("g", (s_cast["wo"], s_cast["wo"].n), writes=[B_wo_c])

        ps_sc = e(nc.psum_tensor("ps_sc", [128, 3072], F32))
        ps_hi = [e(nc.psum_tensor(f"ps{i}", [128, 512], F32))
                 for i in range(6, 8)]
        ps = [ps_sc[:, j * 512:(j + 1) * 512] for j in range(6)] + \
            [p[:] for p in ps_hi]
        B_ps = [Buf(f"ps{i}") for i in range(8)]

        id_sb = e(sb("id_sb", [128, 128], BF16))
        B_id = Buf("id_sb")
        eps_sb = e(sb("eps_sb", [128, 1], F32))
        nhalf_sb = e(sb("nhalf_sb", [128, 1], F32))
        B_eps = Buf("eps"); B_nhalf = Buf("nhalf")
        k.dma("s", s_misc, [(id_sb[:], ident)], writes=[B_id])
        k.on("v", lambda v: (v.memset(eps_sb[:], c.eps),
                             v.memset(nhalf_sb[:], -0.5))[-1],
             writes=[B_eps, B_nhalf])

        def ln_tiles(x_src_rows, xnT_sb, B_xnT_sb, st, bslots, ntiles,
                     ps_lo, ldstream_bufs):
            def chain(i):
                p = i % len(ldstream_bufs)
                xb, B_xb = ldstream_bufs[p]
                k.dma("s", st.sems[p], [(xb[:], x_src_rows(i))],
                      writes=[B_xb])
                stats, B_stats = ln_stats[p % 2]
                mvb, B_mv = ln_mv[p % 2]
                rstd, B_rstd = ln_rstd[p % 2]
                nmr, B_nmr = ln_nmr[p % 2]
                xnb, B_xnb = ln_xn[i % 4]
                nst = ceil_div(c.H, 512)

                def bnst(v, xb=xb, nst=nst, stats=stats):
                    last = None
                    for ii in range(nst):
                        w = min(512, c.H - ii * 512)
                        last = v.bn_stats(
                            out=stats[:, ii * 6:(ii + 1) * 6],
                            in_=xb[:, ii * 512:ii * 512 + w])
                    return last
                k.on("v", bnst, reads=[B_xb], writes=[B_stats])
                k.on("v", lambda v: v.bn_aggr(out=mvb[:], in_=stats[:]),
                     reads=[B_stats], writes=[B_mv])
                k.batch("a", [
                    (lambda a: a.activation(
                        out=rstd[:], in_=mvb[:, 1:2], func=AF.Ln,
                        bias=eps_sb[:, 0:1], scale=1.0),
                     [B_mv, B_eps], [B_rstd]),
                    (lambda a: a.activation(
                        out=rstd[:], in_=rstd[:], func=AF.Exp, bias=0.0,
                        scale=nhalf_sb[:, 0:1]),
                     [B_nhalf], [B_rstd]),
                ])
                k.on("v", lambda v: v.tensor_scalar(
                    out=nmr[:], in0=mvb[:, 0:1], scalar1=rstd[:, 0:1],
                    scalar2=-1.0, op0=ALU.mult, op1=ALU.mult),
                    reads=[B_mv, B_rstd], writes=[B_nmr])
                k.on("v", lambda v, xb=xb: v.tensor_scalar(
                    out=xnb[:], in0=xb[:], scalar1=rstd[:, 0:1],
                    scalar2=nmr[:, 0:1], op0=ALU.mult, op1=ALU.add),
                    reads=[B_xb, B_nmr, B_rstd], writes=[B_xnb])

            def transposes(i):
                xnb, B_xnb = ln_xn[i % 4]
                for j0 in range(0, c.HT, 4):
                    bank = ps_lo + (j0 // 4) % 2
                    nj = min(4, c.HT - j0)

                    def mmb(t_, j0=j0, nj=nj, bank=bank, xnb=xnb):
                        last = None
                        for j in range(nj):
                            last = t_.matmul(
                                ps[bank][:, j * 128:(j + 1) * 128],
                                xnb[:, (j0 + j) * 128:(j0 + j + 1) * 128],
                                id_sb[:], start=True, stop=True)
                        return last
                    k.on("t", mmb, reads=[B_xnb, B_id], writes=[B_ps[bank]])

                    def evac(v, j0=j0, nj=nj, bank=bank, i=i):
                        return v.tensor_copy(
                            out=xnT_sb[:].rearrange(
                                "p (h q) -> p h q", q=bslots * 128)
                            [:, j0:j0 + nj, i * 128:(i + 1) * 128],
                            in_=ps[bank][:, 0:nj * 128].rearrange(
                                "p (h q) -> p h q", q=128))
                    k.on("v", evac, reads=[B_ps[bank]], writes=[B_xnT_sb])

            la = min(3, ntiles)
            for i in range(la):
                chain(i)
            for i in range(ntiles):
                if i + la < ntiles:
                    chain(i + la)
                transposes(i)

        with contextlib.ExitStack() as PA:
            pe = PA.enter_context
            xbufs = [(pe(sb(f"a1_x{j}", [128, c.H], F32)),
                      Buf(f"a1_x{j}")) for j in range(4)]
            ln_stats = [(pe(sb(f"ln_st{j}", [128, 6 * ceil_div(c.H, 512)],
                                F32)), Buf(f"lns{j}")) for j in range(2)]
            ln_mv = [(pe(sb(f"ln_mv{j}", [128, 2], F32)), Buf(f"lnm{j}"))
                     for j in range(2)]
            ln_rstd = [(pe(sb(f"ln_rs{j}", [128, 1], F32)), Buf(f"lnr{j}"))
                       for j in range(2)]
            ln_nmr = [(pe(sb(f"ln_nm{j}", [128, 1], F32)), Buf(f"lnn{j}"))
                      for j in range(2)]
            ln_xn = [(pe(sb(f"ln_xn{j}", [128, c.H], BF16)),
                      Buf(f"lnx{j}")) for j in range(4)]
            xt0 = pe(sb("a1_t0", [128, c.HT * c.CH], BF16))
            xt1 = pe(sb("a1_t1", [128, c.HT * c.CH], BF16))
            xts = [(xt0, Buf("a1_t0")), (xt1, Buf("a1_t1"))]

            for kc in range(c.SC):
                xnT_sb, B_xnT_sb = xts[kc % 2]

                def xsrc(i, kc=kc):
                    r = kc * c.CH + i * 128
                    return xfull[r:r + 128, :]
                ln_tiles(xsrc, xnT_sb, B_xnT_sb, st_x, c.CT, c.CT, 0, xbufs)
                k.dma("s", st_sp.sems[kc % 2],
                      [(xnT_h[kc], xnT_sb[:])],
                      reads=[B_xnT_sb], writes=[B_xnT[kc]])

            k.barrier(dma_sems=[st_x.sems[0], st_x.sems[1],
                                st_sp.sems[0], st_sp.sems[1], s_misc])

        with contextlib.ExitStack() as PA:
            pe = PA.enter_context
            wk_sb = pe(sb("wk_sb", [128, c.HT * c.KVD], BF16))
            B_wk_sb = Buf("wk_sb")
            ck = pe(sb("ck", [64, c.S], F32)); B_ck = Buf("ck")
            sk = pe(sb("sk", [64, c.S], F32)); B_sk = Buf("sk")
            nk = pe(sb("nk", [64, c.S], F32)); B_nk = Buf("nk")
            k.dma("s", s_misc, [(wk_sb[:], wk_c), (ck[:], cosk),
                                (sk[:], sink), (nk[:], nsink)],
                  reads=[B_wk_c],
                  writes=[B_wk_sb, B_ck, B_sk, B_nk])
            xt0 = pe(sb("a2_t0", [128, c.HT * c.CH], BF16))
            xt1 = pe(sb("a2_t1", [128, c.HT * c.CH], BF16))
            xts = [(xt0, Buf("a2_t0")), (xt1, Buf("a2_t1"))]
            tmpA = pe(sb("rp_a", [128, c.CH], F32))
            tmpB = pe(sb("rp_b", [128, c.CH], F32))
            B_tA, B_tB = Buf("rp_a"), Buf("rp_b")
            kst0 = pe(sb("kst0", [128, c.CH], BF16))
            kst1 = pe(sb("kst1", [128, c.CH], BF16))
            ksts = [(kst0, Buf("kst0")), (kst1, Buf("kst1"))]

            for kc in range(c.SC):
                xnT_sb, B_xnT_sb = xts[kc % 2]
                k.dma("s", st_x.sems[kc % 2], [(xnT_sb[:], xnT_h[kc])],
                      reads=[B_xnT[kc]], writes=[B_xnT_sb])
                for m in range(c.KVT):
                    bank = m % 2

                    def mmk(t_, m=m, xnT_sb=xnT_sb):
                        last = None
                        for t in range(c.HT):
                            last = t_.matmul(
                                ps[bank],
                                wk_sb[:, t * c.KVD + m * 128:
                                      t * c.KVD + (m + 1) * 128],
                                xnT_sb[:, t * c.CH:(t + 1) * c.CH],
                                start=(t == 0), stop=(t == c.HT - 1))
                        return last
                    k.on("t", mmk, reads=[B_wk_sb, B_xnT_sb],
                         writes=[B_ps[bank]])
                    kcs = slice(kc * c.CH, (kc + 1) * c.CH)

                    def rope1(v, bank=bank, kcs=kcs):
                        v.tensor_mul(tmpA[0:64, :], ps[bank][0:64, :],
                                     ck[:, kcs])
                        v.tensor_mul(tmpA[64:128, :], ps[bank][64:128, :],
                                     ck[:, kcs])
                        v.tensor_mul(tmpB[0:64, :], ps[bank][64:128, :],
                                     nk[:, kcs])
                        return v.tensor_mul(tmpB[64:128, :], ps[bank][0:64, :],
                                            sk[:, kcs])
                    k.on("v", rope1, reads=[B_ps[bank], B_ck, B_sk, B_nk],
                         writes=[B_tA, B_tB])
                    kst, B_kst = ksts[m % 2]
                    k.on("v", lambda v: v.tensor_add(kst[:], tmpA[:], tmpB[:]),
                         reads=[B_tA, B_tB], writes=[B_kst])
                    k.dma("s", st_sp.sems[m % 2],
                          [(kt_h[m][:, kc * c.CH:(kc + 1) * c.CH], kst[:])],
                          reads=[B_kst], writes=[B_kt])
            k.barrier(dma_sems=[st_x.sems[0], st_x.sems[1],
                                st_sp.sems[0], st_sp.sems[1], s_misc])

        with contextlib.ExitStack() as PA:
            pe = PA.enter_context
            wv_sb = pe(sb("wv_sb", [128, c.HT * c.KVD], BF16))
            B_wv_sb = Buf("wv_sb")
            k.dma("s", s_misc, [(wv_sb[:], wv_c)], reads=[B_wv_c],
                  writes=[B_wv_sb])
            xt0 = pe(sb("a3_t0", [128, c.HT * c.CH], BF16))
            xt1 = pe(sb("a3_t1", [128, c.HT * c.CH], BF16))
            xts = [(xt0, Buf("a3_t0")), (xt1, Buf("a3_t1"))]
            vst0 = pe(sb("vst0", [128, c.KVD], BF16))
            vst1 = pe(sb("vst1", [128, c.KVD], BF16))
            vsts = [(vst0, Buf("vst0")), (vst1, Buf("vst1"))]

            for kc in range(c.SC):
                xnT_sb, B_xnT_sb = xts[kc % 2]
                k.dma("s", st_x.sems[kc % 2], [(xnT_sb[:], xnT_h[kc])],
                      reads=[B_xnT[kc]], writes=[B_xnT_sb])
                for t4 in range(c.CT):
                    vst, B_vst = vsts[t4 % 2]
                    nhalf = ceil_div(c.KVD, 512)
                    for n in range(nhalf):
                        bank = n % 2
                        nw = min(512, c.KVD - n * 512)

                        def mmv(t_, t4=t4, n=n, nw=nw, bank=bank,
                                xnT_sb=xnT_sb):
                            last = None
                            for t in range(c.HT):
                                last = t_.matmul(
                                    ps[bank][:, 0:nw],
                                    xnT_sb[:, t * c.CH + t4 * 128:
                                           t * c.CH + (t4 + 1) * 128],
                                    wv_sb[:, t * c.KVD + n * 512:
                                          t * c.KVD + n * 512 + nw],
                                    start=(t == 0), stop=(t == c.HT - 1))
                            return last
                        k.on("t", mmv, reads=[B_wv_sb, B_xnT_sb],
                             writes=[B_ps[bank]])
                        k.on("v", lambda v, n=n, nw=nw, bank=bank:
                             v.tensor_copy(out=vst[:, n * 512:n * 512 + nw],
                                           in_=ps[bank][:, 0:nw]),
                             reads=[B_ps[bank]], writes=[B_vst])
                    row = kc * c.CH + t4 * 128
                    k.dma("s", st_sp.sems[t4 % 2],
                          [(v_h[g][row:row + 128, :],
                            vst[:, g * 128:(g + 1) * 128])
                           for g in range(c.NHKV)],
                          reads=[B_vst], writes=[B_v])
            k.barrier(dma_sems=[st_x.sems[0], st_x.sems[1],
                                st_sp.sems[0], st_sp.sems[1], s_misc])

        gs_sb = e(sb("gs_sb", [128, c.IT], F32))
        us_sb = e(sb("us_sb", [128, c.IT], F32))
        ds_sb = e(sb("ds_sb", [128, c.HT], F32))
        B_gs, B_us, B_ds = Buf("gs"), Buf("us"), Buf("ds")
        k.dma("s", s_misc,
              [(gs_sb[:], gs), (us_sb[:], us), (ds_sb[:], ds)],
              writes=[B_gs, B_us, B_ds])

        for ci in range(c.NC):
            bs = c.slot_blocks[ci]
            nkk = bs * (512 // 128)
            with contextlib.ExitStack() as PC:
                pe = PC.enter_context
                QT = pe(nc.sbuf_tensor("b_QT", [128, c.NHQ * c.CH], BF16,
                                       side="right",
                                       allow_name_mangling=True))
                B_QT = Buf("b_QT")
                xnT_cm = nc.sbuf_tensor("b_xnT", [128, c.HT * c.CH], BF16,
                                        side="right",
                                        allow_name_mangling=True)
                xnT_sb = xnT_cm.__enter__()
                B_xnT_sb = Buf("b_xnT")

                with contextlib.ExitStack() as P0:
                    p0 = P0.enter_context
                    xbufs = [(p0(sb(f"b0_x{j}", [128, c.H], F32)),
                              Buf(f"b0_x{j}")) for j in range(4)]
                    ln_stats = [(p0(sb(f"b0_st{j}",
                                       [128, 6 * ceil_div(c.H, 512)], F32)),
                                 Buf(f"b0s{j}")) for j in range(2)]
                    ln_mv = [(p0(sb(f"b0_mv{j}", [128, 2], F32)),
                              Buf(f"b0m{j}")) for j in range(2)]
                    ln_rstd = [(p0(sb(f"b0_rs{j}", [128, 1], F32)),
                                Buf(f"b0r{j}")) for j in range(2)]
                    ln_nmr = [(p0(sb(f"b0_nm{j}", [128, 1], F32)),
                               Buf(f"b0n{j}")) for j in range(2)]
                    ln_xn = [(p0(sb(f"b0_xn{j}", [128, c.H], BF16)),
                              Buf(f"b0x{j}")) for j in range(4)]

                    def xsrc(i, ci=ci):
                        r = ci * c.CH + i * 128
                        return xq[r:r + 128, :]
                    ln_tiles(xsrc, xnT_sb, B_xnT_sb, st_x, c.CT, c.CT, 0,
                             xbufs)
                    k.barrier(dma_sems=[st_x.sems[0], st_x.sems[1]])

                with contextlib.ExitStack() as P1:
                    p1 = P1.enter_context
                    interT = p1(sb("interT",
                                               [128, c.IT * c.CH], BF16))
                    B_inter = Buf("interT")
                    P1a = p1(contextlib.ExitStack())
                    p1a = P1a.enter_context
                    gsl0 = p1a(sb("b1_g0", [128, c.HT * 128], BF16))
                    gsl1 = p1a(sb("b1_g1", [128, c.HT * 128], BF16))
                    usl0 = p1a(sb("b1_u0", [128, c.HT * 128], BF16))
                    usl1 = p1a(sb("b1_u1", [128, c.HT * 128], BF16))
                    gslabs = [(gsl0, Buf("b1_g0")), (gsl1, Buf("b1_g1"))]
                    uslabs = [(usl0, Buf("b1_u0")), (usl1, Buf("b1_u1"))]
                    gsts = [(p1a(sb(f"b1_g{j}", [128, c.CH], BF16)),
                             Buf(f"b1_g{j}")) for j in range(4)]

                    def mm(t_, slab, bank):
                        last = None
                        for t in range(c.HT):
                            last = t_.matmul(
                                ps[bank],
                                slab[:, t * 128:(t + 1) * 128],
                                xnT_sb[:, t * c.CH:(t + 1) * c.CH],
                                start=(t == 0), stop=(t == c.HT - 1))
                        return last

                    for m0 in range(0, c.IT, 4):
                        ng = min(4, c.IT - m0)
                        for m in range(m0, m0 + ng):
                            j = m - m0
                            p = m % 2
                            gslab, B_gslab = gslabs[p]
                            uslab, B_uslab = uslabs[p]
                            k.dma("s", st_w.sems[p], [(gslab[:], g_c[m])],
                                  reads=[B_g_c], writes=[B_gslab])
                            k.dma("s", st_f.sems[p], [(uslab[:], u_c[m])],
                                  reads=[B_u_c], writes=[B_uslab])
                            k.on("t", lambda t_, s=gslab, b=j: mm(t_, s, b),
                                 reads=[B_gslab, B_xnT_sb],
                                 writes=[B_ps[j]])
                            k.on("t", lambda t_, s=uslab, b=4 + j:
                                 mm(t_, s, b),
                                 reads=[B_uslab, B_xnT_sb],
                                 writes=[B_ps[4 + j]])
                        items = []
                        for m in range(m0, m0 + ng):
                            j = m - m0
                            gstb, B_gstb = gsts[j]

                            def dosilu(a, b=j, gstb=gstb, m=m):
                                return a.activation(
                                    out=gstb[:], in_=ps[b],
                                    func=AF.Silu, bias=0.0,
                                    scale=gs_sb[:, m:m + 1])
                            items.append((dosilu, [B_ps[j], B_gs],
                                          [B_gstb]))
                        k.batch("a", items)
                        items = []
                        for m in range(m0, m0 + ng):
                            j = m - m0
                            gstb, B_gstb = gsts[j]

                            def dostt(v, b=4 + j, gstb=gstb, m=m):
                                return v.scalar_tensor_tensor(
                                    out=interT[:, m * c.CH:(m + 1) * c.CH],
                                    in0=ps[b],
                                    scalar=us_sb[:, m:m + 1],
                                    in1=gstb[:], op0=ALU.mult, op1=ALU.mult)
                            items.append((dostt, [B_ps[4 + j], B_us, B_gstb],
                                          [B_inter]))
                        k.batch("v", items)
                    k.barrier(dma_sems=[st_w.sems[0], st_w.sems[1],
                                        st_f.sems[0], st_f.sems[1]])
                    P1a.close()

                    with contextlib.ExitStack() as P3:
                        p3 = P3.enter_context
                        sl0 = p3(sb("b3_s0", [128, c.HT * 128], BF16))
                        sl1 = p3(sb("b3_s1", [128, c.HT * 128], BF16))
                        slabs = [(sl0, Buf("b3_s0")), (sl1, Buf("b3_s1"))]
                        cqb = p3(sb("cq", [64, c.CH], F32))
                        sqb = p3(sb("sq", [64, c.CH], F32))
                        nqb = p3(sb("nq", [64, c.CH], F32))
                        B_cq, B_sq, B_nq = Buf("cq"), Buf("sq"), Buf("nq")
                        qsl = slice(ci * c.CH, (ci + 1) * c.CH)
                        k.dma("s", s_misc, [(cqb[:], cosq[:, qsl]),
                                            (sqb[:], sinq[:, qsl]),
                                            (nqb[:], nsinq[:, qsl])],
                              writes=[B_cq, B_sq, B_nq])
                        tmpA = p3(sb("b3_a", [128, c.CH], F32))
                        tmpB = p3(sb("b3_b", [128, c.CH], F32))
                        B_tA, B_tB = Buf("b3_a"), Buf("b3_b")

                        for m in range(c.NHQ):
                            p = m % 2
                            slab, B_slab = slabs[p]
                            k.dma("s", st_w.sems[p], [(slab[:], wq_c[m])],
                                  reads=[B_wq_c], writes=[B_slab])
                            bank = m % 2

                            def mm(t_, slab=slab, bank=bank):
                                last = None
                                for t in range(c.HT):
                                    last = t_.matmul(
                                        ps[bank],
                                        slab[:, t * 128:(t + 1) * 128],
                                        xnT_sb[:, t * c.CH:(t + 1) * c.CH],
                                        start=(t == 0), stop=(t == c.HT - 1))
                                return last
                            k.on("t", mm, reads=[B_slab, B_xnT_sb],
                                 writes=[B_ps[bank]])

                            def rope1(v, bank=bank):
                                v.tensor_mul(tmpA[0:64, :], ps[bank][0:64, :],
                                             cqb[:])
                                v.tensor_mul(tmpA[64:128, :],
                                             ps[bank][64:128, :], cqb[:])
                                v.tensor_mul(tmpB[0:64, :],
                                             ps[bank][64:128, :], nqb[:])
                                return v.tensor_mul(tmpB[64:128, :],
                                                    ps[bank][0:64, :],
                                                    sqb[:])
                            k.on("v", rope1,
                                 reads=[B_ps[bank], B_cq, B_sq, B_nq],
                                 writes=[B_tA, B_tB])
                            k.on("v", lambda v, m=m: v.tensor_add(
                                QT[:, m * c.CH:(m + 1) * c.CH], tmpA[:], tmpB[:]),
                                reads=[B_tA, B_tB], writes=[B_QT])
                        k.barrier(dma_sems=[st_w.sems[0], st_w.sems[1], s_misc])

                    xnT_cm.__exit__(None, None, None)

                    with contextlib.ExitStack() as P2:
                        p2 = P2.enter_context
                        dslabs = [(p2(sb(f"b2_s{j}", [128, c.IT * 128],
                                         BF16)), Buf(f"b2_s{j}"))
                                  for j in range(3)]
                        dslab2s = [Buf(f"b2_s{j}h") for j in range(3)]
                        fst0 = p2(sb("b2_f0", [128, c.CH], F32))
                        fst1 = p2(sb("b2_f1", [128, c.CH], F32))
                        fsts = [(fst0, Buf("b2_f0")), (fst1, Buf("b2_f1"))]
                        for m in range(c.HT):
                            p = m % 3
                            slab, B_slab = dslabs[p]
                            h1 = c.IT // 2 * 128
                            B_slab2 = dslab2s[p]
                            k.dma("s", st_w.sems[p],
                                  [(slab[:, 0:h1], d_c[m, :, 0:h1])],
                                  reads=[B_d_c], writes=[B_slab])
                            k.dma("g", st_w2.sems[p],
                                  [(slab[:, h1:], d_c[m, :, h1:])],
                                  reads=[B_d_c], writes=[B_slab2])
                            bank = m % 2 + 2
                            kh = c.IT // 2

                            def mm1(t_, slab=slab, bank=bank, kh=kh):
                                last = None
                                for t in range(kh):
                                    last = t_.matmul(
                                        ps[bank],
                                        slab[:, t * 128:(t + 1) * 128],
                                        interT[:, t * c.CH:(t + 1) * c.CH],
                                        start=(t == 0), stop=False)
                                return last
                            k.on("t", mm1, reads=[B_slab, B_inter],
                                 writes=[B_ps[bank]])

                            def mm2(t_, slab=slab, bank=bank, kh=kh):
                                last = None
                                for t in range(kh, c.IT):
                                    last = t_.matmul(
                                        ps[bank],
                                        slab[:, t * 128:(t + 1) * 128],
                                        interT[:, t * c.CH:(t + 1) * c.CH],
                                        start=False, stop=(t == c.IT - 1))
                                return last
                            k.on("t", mm2, reads=[B_slab2, B_inter],
                                 writes=[B_ps[bank]])
                            fst, B_fst = fsts[m % 2]
                            k.on("v", lambda v, bank=bank, fst=fst, m=m:
                                 v.tensor_scalar(
                                     out=fst[:], in0=ps[bank],
                                     scalar1=ds_sb[:, m:m + 1], scalar2=None,
                                     op0=ALU.mult),
                                 reads=[B_ps[bank], B_ds], writes=[B_fst])
                            k.dma("s", st_sp.sems[m % 2],
                                  [(ffn_h[ci, m], fst[:])],
                                  reads=[B_fst], writes=[B_ffn[ci]])
                        k.barrier(dma_sems=[st_w.sems[0], st_w.sems[1],
                                            st_w2.sems[0], st_w2.sems[1],
                                            st_sp.sems[0], st_sp.sems[1]])

                aoT = pe(sb("b_aoT", [128, c.NHQ * c.CH], BF16))
                B_aoT = Buf("b_aoT")
                with contextlib.ExitStack() as P4:
                    p4 = P4.enter_context
                    mk_sb = p4(sb("mk_sb", [128, 8 * 512], BF16))
                    B_mk = Buf("mk_sb")
                    k.dma("g", s_cast["kv"],
                          [(mk_sb[:].rearrange("p (n w) -> p n w", w=512),
                            maskT[ci * 8 * 128:(ci + 1) * 8 * 128, :]
                            .rearrange("(n p) w -> p n w", p=128))],
                          writes=[B_mk])
                    kt0 = p4(sb("b4_k0", [128, bs * 512], BF16))
                    kt1 = p4(sb("b4_k1", [128, bs * 512], BF16))
                    kts = [(kt0, Buf("b4_k0")), (kt1, Buf("b4_k1"))]
                    vt0 = p4(sb("b4_v0", [128, nkk * 128], BF16))
                    vt1 = p4(sb("b4_v1", [128, nkk * 128], BF16))
                    vts = [(vt0, Buf("b4_v0")), (vt1, Buf("b4_v1"))]
                    natT = 3 if bs <= 6 else 2
                    atT2 = [(p4(sb(f"b4_aT{j}", [128, nkk * 512], BF16)),
                             Buf(f"b4_aT{j}")) for j in range(natT)]
                    b20 = p4(sb("b4_b20", [128, 1], F32))
                    B_b20 = Buf("b4_b20")
                    ones_col = p4(sb("b4_on", [128, 1], BF16))
                    B_ones = Buf("b4_on")
                    ones_row = p4(sb("b4_or", [1, 128], BF16))
                    B_onr = Buf("b4_or")
                    rcs = [dict(
                        rc_bf=p4(sb(f"b4_rc{j}", [1, 512], BF16)),
                        B_rcb=Buf(f"b4_rc{j}"),
                        rc_f=p4(sb(f"b4_rf{j}", [1, 512], F32)),
                        B_rcf=Buf(f"b4_rf{j}"),
                        sm_row=p4(sb(f"b4_sm{j}", [1, 512], F32)),
                        B_smr=Buf(f"b4_sm{j}")) for j in range(2)]
                    rcx = p4(sb("b4_rx", [128, 512], BF16))
                    B_rcx = Buf("b4_rx")

                    def minit(v):
                        v.memset(b20[:], -20.0)
                        v.memset(ones_col[:], 1.0)
                        return v.memset(ones_row[:], 1.0)
                    k.on("v", minit, writes=[B_b20, B_ones, B_onr])

                    def kv_load(g):
                        pkv = g % 2
                        ktb, B_ktb = kts[pkv]
                        vtb, B_vtb = vts[pkv]
                        k.dma("s", st_kv.sems[pkv],
                              [(ktb[:], kt_h[g][:, 0:bs * 512]),
                               (vtb[:].rearrange("p (t d) -> p t d", d=128),
                                v_h[g][0:nkk * 128, :].rearrange(
                                    "(t p) d -> p t d", p=128))],
                              reads=[B_kt, B_v], writes=[B_ktb, B_vtb])

                    ngrp = ceil_div(nkk, 3)
                    asem = k.esem["a"]
                    abase = asem.n
                    exp_jobs = []
                    nexp = [0]
                    parity_last = {0: None, 1: None}
                    last_exp_tok = [None]

                    def sc_exp(g, h, hi):
                        ktb, B_ktb = kts[g % 2]
                        atT, B_atT = atT2[hi % natT]
                        for gi in range(ngrp):
                            k0 = gi * 3
                            ng = min(3, nkk - k0)
                            b0 = (gi % 2) * 3
                            pt = parity_last[gi % 2]
                            wait_toks = []
                            for kk in range(k0, k0 + ng):
                                bank = b0 + (kk - k0)

                                def mmsc(t_, h=h, kk=kk, bank=bank,
                                         ktb=ktb, pt=pt):
                                    if pt is not None:
                                        k._wait(t_, "t", pt)
                                    return t_.matmul(
                                        ps[bank],
                                        ktb[:, kk * 128:(kk + 1) * 128],
                                        QT[:, h * c.CH:(h + 1) * c.CH],
                                        start=True, stop=True)
                                tok = k.on("t", mmsc, reads=[B_QT, B_ktb],
                                           writes=[B_ps[bank]])
                                wait_toks.append(tok)
                                if kk >= nkk - 8:
                                    j = kk - (nkk - 8)

                                    def msk(v, j=j, bank=bank, pt=pt):
                                        if pt is not None:
                                            k._wait(v, "v", pt)
                                        return v.tensor_add(
                                            ps[bank], ps[bank],
                                            mk_sb[:, j * 512:(j + 1) * 512])
                                    mtok = k.on("v", msk, reads=[B_mk],
                                                writes=[B_ps[bank]])
                                    wait_toks[-1] = mtok
                            if hi >= natT and gi == 0:
                                wait_toks = wait_toks + \
                                    list(atT2[hi % natT][1].readers)
                            wait_toks = wait_toks + list(B_b20.writers)
                            exp_jobs.append((wait_toks, k0, ng, b0, atT))
                            nexp[0] += 1
                            parity_last[gi % 2] = (asem, abase + nexp[0])
                            if gi == ngrp - 1:
                                last_exp_tok[0] = (asem, abase + nexp[0])

                    def flush_exps():
                        jobs = list(exp_jobs)
                        exp_jobs.clear()

                        def _f(a):
                            for wait_toks, k0, ng, b0, atT in jobs:
                                for tok in wait_toks:
                                    k._wait(a, "a", tok)
                                a.activation(
                                    out=atT[:, k0 * 512:(k0 + ng) * 512],
                                    in_=ps_sc[:, b0 * 512:(b0 + ng) * 512],
                                    func=AF.Exp, bias=b20[:, 0:1],
                                    scale=1.0).then_inc(asem.h, 1)
                                asem.n += 1
                        blk.scalar(_f)

                    lastexp_by_hi = {}

                    def sum_part(g, h, qh, hi):
                        atT, B_atT = atT2[hi % natT]
                        rr = rcs[hi % 2]
                        sm_row, B_smr = rr["sm_row"], rr["B_smr"]
                        rc_f, B_rcf = rr["rc_f"], rr["B_rcf"]
                        rc_bf, B_rcb = rr["rc_bf"], rr["B_rcb"]
                        lastexp = lastexp_by_hi[hi]

                        def mmsum(t_, atT=atT, lastexp=lastexp):
                            k._wait(t_, "t", lastexp)
                            last = None
                            for kk in range(nkk):
                                last = t_.matmul(
                                    ps[6][0:1, :], ones_col[:],
                                    atT[:, kk * 512:(kk + 1) * 512],
                                    start=(kk == 0), stop=(kk == nkk - 1))
                            return last
                        k.on("t", mmsum, reads=[B_ones], writes=[B_ps[6]])
                        k.on("v", lambda v: v.tensor_copy(
                            out=sm_row[:], in_=ps[6][0:1, :]),
                            reads=[B_ps[6]], writes=[B_smr])
                        k.on("v", lambda v: v.reciprocal(
                            out=rc_f[:], in_=sm_row[:]),
                            reads=[B_smr], writes=[B_rcf])
                        k.on("v", lambda v: v.tensor_copy(out=rc_bf[:],
                                                          in_=rc_f[:]),
                             reads=[B_rcf], writes=[B_rcb])

                    def av_part(g, h, qh, hi):
                        vtb, B_vtb = vts[g % 2]
                        atT, B_atT = atT2[hi % natT]
                        rr = rcs[hi % 2]
                        rc_bf, B_rcb = rr["rc_bf"], rr["B_rcb"]
                        k.on("t", lambda t_: t_.matmul(
                            ps[6], ones_row[:], rc_bf[:],
                            start=True, stop=True),
                            reads=[B_rcb, B_onr], writes=[B_ps[6]])
                        k.on("v", lambda v: v.tensor_copy(
                            out=rcx[:], in_=ps[6]),
                            reads=[B_ps[6]], writes=[B_rcx])

                        def mmav(t_, vtb=vtb, atT=atT):
                            last = None
                            for kk in range(nkk):
                                last = t_.matmul(
                                    ps[7],
                                    vtb[:, kk * 128:(kk + 1) * 128],
                                    atT[:, kk * 512:(kk + 1) * 512],
                                    start=(kk == 0), stop=(kk == nkk - 1))
                            return last
                        k.on("t", mmav, reads=[B_vtb, B_atT],
                             writes=[B_ps[7]])
                        k.on("v", lambda v, h=h: v.tensor_mul(
                            aoT[:, h * c.CH:(h + 1) * c.CH],
                            ps[7], rcx[:]),
                            reads=[B_ps[7], B_rcx], writes=[B_aoT])

                    heads = [(g, g * c.GRP + qh, qh)
                             for g in range(c.NHKV)
                             for qh in range(c.GRP)]
                    kv_load(0)
                    for hi, (g, h, qh) in enumerate(heads):
                        if qh == 0 and g + 1 < c.NHKV:
                            kv_load(g + 1)
                        sc_exp(g, h, hi)
                        lastexp_by_hi[hi] = last_exp_tok[0]
                        flush_exps()
                        if hi >= 1:
                            sum_part(*heads[hi - 1], hi - 1)
                        if natT >= 3:
                            if hi >= 2:
                                av_part(*heads[hi - 2], hi - 2)
                        else:
                            if hi >= 1:
                                av_part(*heads[hi - 1], hi - 1)
                    if natT >= 3:
                        sum_part(*heads[-1], len(heads) - 1)
                        av_part(*heads[-2], len(heads) - 2)
                        av_part(*heads[-1], len(heads) - 1)
                    else:
                        sum_part(*heads[-1], len(heads) - 1)
                        av_part(*heads[-1], len(heads) - 1)
                    assert asem.n == abase + nexp[0]
                    k.barrier(dma_sems=[st_kv.sems[0], st_kv.sems[1],
                                        s_misc])

                with contextlib.ExitStack() as P5:
                    p5 = P5.enter_context
                    slabs = [(p5(sb(f"b5_s{j}", [128, c.HT * 128], BF16)),
                              Buf(f"b5_s{j}")) for j in range(4)]
                    xqt_sb = p5(sb("b5_xqt",
                                               [128, c.HT * c.CH], F32))
                    B_xqt = Buf("b5_xqt")
                    k.dma("s", s_misc,
                          [(xqt_sb[:].rearrange("p (t q) -> p t q", q=c.CH),
                            xqT[:, ci * c.CH:(ci + 1) * c.CH].rearrange(
                                "(t p) q -> p t q", p=128))],
                          writes=[B_xqt])
                    fb0 = p5(sb("b5_f0", [128, c.CH], F32))
                    fb1 = p5(sb("b5_f1", [128, c.CH], F32))
                    fbs = [(fb0, Buf("b5_f0")), (fb1, Buf("b5_f1"))]
                    ob0 = p5(sb("b5_o0", [128, c.CH], F32))
                    ob1 = p5(sb("b5_o1", [128, c.CH], F32))
                    obs = [(ob0, Buf("b5_o0")), (ob1, Buf("b5_o1"))]
                    tb0 = p5(sb("b5_t0", [128, c.CH], F32))
                    tb1 = p5(sb("b5_t1", [128, c.CH], F32))
                    tbs = [(tb0, Buf("b5_t0")), (tb1, Buf("b5_t1"))]

                    for m in range(c.HT):
                        p4i = m % 4
                        slab, B_slab = slabs[p4i]
                        if p4i < 2:
                            k.dma("s", st_w.sems[p4i], [(slab[:], wo_c[m])],
                                  reads=[B_wo_c], writes=[B_slab])
                        else:
                            k.dma("g", st_w2.sems[p4i - 2],
                                  [(slab[:], wo_c[m])],
                                  reads=[B_wo_c], writes=[B_slab])
                        fb, B_fb = fbs[m % 2]
                        k.dma("s", st_f.sems[m % 2], [(fb[:], ffn_h[ci, m])],
                              reads=[B_ffn[ci]], writes=[B_fb])
                        bank = m % 2

                        def mm(t_, slab=slab, bank=bank):
                            last = None
                            for t in range(c.HT):
                                last = t_.matmul(
                                    ps[bank],
                                    slab[:, t * 128:(t + 1) * 128],
                                    aoT[:, t * c.CH:(t + 1) * c.CH],
                                    start=(t == 0), stop=(t == c.HT - 1))
                            return last
                        k.on("t", mm, reads=[B_slab, B_aoT],
                             writes=[B_ps[bank]])
                        tb, B_tb = tbs[m % 2]
                        ob, B_ob = obs[m % 2]
                        k.on("v", lambda v, bank=bank, tb=tb, fb=fb:
                             v.tensor_add(tb[:], ps[bank], fb[:]),
                             reads=[B_ps[bank], B_fb], writes=[B_tb])
                        k.on("v", lambda v, tb=tb, ob=ob, m=m: v.tensor_add(
                            ob[:], tb[:],
                            xqt_sb[:, m * c.CH:(m + 1) * c.CH]),
                            reads=[B_tb, B_xqt], writes=[B_ob])
                        k.dma("s", st_out.sems[m % 2],
                              [(outT[m * 128:(m + 1) * 128,
                                     ci * c.CH:(ci + 1) * c.CH], ob[:])],
                              reads=[B_ob], writes=[B_out])
                    k.barrier(dma_sems=[st_w.sems[0], st_w.sems[1],
                                        st_w2.sems[0], st_w2.sems[1],
                                        st_f.sems[0], st_f.sems[1], s_misc,
                                        st_out.sems[0], st_out.sems[1]])

        def _fin(eng):
            eng.wait_ge(st_out.sems[0].h, st_out.sems[0].n)
            eng.wait_ge(st_out.sems[1].h, st_out.sems[1].n)
        blk.sync(_fin)

    return nc


def _rope_tables(n, D, theta, scale):
    inv = 1.0 / theta ** (np.arange(0, D, 2, dtype=np.float64)[: D // 2] / D)
    freqs = np.arange(n, dtype=np.float64)[None, :] * inv[:, None]
    cos = (np.cos(freqs) * scale).astype(np.float32)
    sin = (np.sin(freqs) * scale).astype(np.float32)
    return cos, sin


def _mask_blocks(cfg, positions):
    c = cfg
    out = np.zeros((c.NC, 8, 128, 512), np.float32)
    for s in range(c.NC):
        bsl = c.slot_blocks[s]
        nkk = bsl * 4
        p = positions[s]
        qg = p * c.CH + np.arange(c.CH)[None, :]
        for j in range(8):
            kk = nkk - 8 + j
            keyg = kk * 128 + np.arange(128)[:, None]
            out[s, j] = np.where(keyg > qg, NEG, 0.0)
    return out


def make_in_maps(cfg, inputs):
    c = cfg
    x = np.ascontiguousarray(inputs["L0_attn_x"], np.float32)
    wq = np.ascontiguousarray(inputs["L0_attn_Wq"], np.float32)
    wk = np.ascontiguousarray(inputs["L0_attn_Wk"], np.float32)
    wv = np.ascontiguousarray(inputs["L0_attn_Wv"], np.float32)
    wo = np.ascontiguousarray(inputs["L0_attn_Wo"], np.float32)
    gwT = np.ascontiguousarray(inputs["L0_ffn_gate_wq"].T).astype(np.int32)
    uwT = np.ascontiguousarray(inputs["L0_ffn_up_wq"].T).astype(np.int32)
    dwT = np.ascontiguousarray(inputs["L0_ffn_down_wq"].T).astype(np.int32)
    gs = np.ascontiguousarray(
        np.asarray(inputs["L0_ffn_gate_scale"], np.float32)
        .reshape(c.IT, 128).T)
    us = np.ascontiguousarray(
        np.asarray(inputs["L0_ffn_up_scale"], np.float32)
        .reshape(c.IT, 128).T)
    ds = np.ascontiguousarray(
        np.asarray(inputs["L0_ffn_down_scale"], np.float32)
        .reshape(c.HT, 128).T)
    cosk, sink = _rope_tables(c.S, c.D, c.theta, 1.0)
    cosqf, sinqf = _rope_tables(c.S, c.D, c.theta, 1.0 / np.sqrt(c.D))
    ident = np.eye(128).astype(ml_dtypes.bfloat16)

    in_maps = []
    for core in range(c.n_cores):
        b = core // 2
        grp = core % 2
        positions = c.posX if grp == 0 else c.posY
        rows = np.concatenate(
            [np.arange(p * c.CH, (p + 1) * c.CH) for p in positions])
        xb = x[b]
        in_maps.append(dict(
            xfull=xb,
            xq=np.ascontiguousarray(xb[rows]),
            xqT=np.ascontiguousarray(xb[rows].T),
            wq=wq, wk=wk, wv=wv, wo=wo,
            gwT=gwT, uwT=uwT, dwT=dwT, gs=gs, us=us, ds=ds,
            cosk=cosk, sink=sink, nsink=-sink,
            cosq=np.ascontiguousarray(cosqf[:, rows]),
            sinq=np.ascontiguousarray(sinqf[:, rows]),
            nsinq=np.ascontiguousarray(-sinqf[:, rows]),
            maskT=_mask_blocks(c, positions).reshape(c.NC * 8 * 128, 512),
            ident=ident,
        ))
    return in_maps


def assemble(cfg, results):
    c = cfg
    out = np.empty((c.B, c.S, c.H), np.float32)
    for core, r in enumerate(results):
        b = core // 2
        grp = core % 2
        positions = c.posX if grp == 0 else c.posY
        oT = np.asarray(r["outT"])
        for s, p in enumerate(positions):
            out[b, p * c.CH:(p + 1) * c.CH, :] = \
                oT[:, s * c.CH:(s + 1) * c.CH].T
    return out


def kernel(**inputs):
    cfg = Cfg()
    nc = build_program(cfg)
    in_maps = make_in_maps(cfg, inputs)
    res = run_bass_kernel_spmd(nc, in_maps, core_ids=list(range(cfg.n_cores)))
    return assemble(cfg, res.results)


# version 15
# speedup vs baseline: 1.3221x; 1.3221x over previous
import contextlib
import numpy as np
import ml_dtypes
import concourse.bass as bass
import concourse.mybir as mybir
from concourse.bass_utils import run_bass_kernel_spmd

F32 = mybir.dt.float32
BF16 = mybir.dt.bfloat16
I32 = mybir.dt.int32
AF = mybir.ActivationFunctionType
ALU = mybir.AluOpType
AX = mybir.AxisListType

NEG = -1.0e30


class Cfg:
    def __init__(self, H=4096, I=11008, NHQ=32, NHKV=8, B=4, S=4096,
                 n_cores=8, theta=10000.0, eps=1e-5):
        self.H, self.I, self.NHQ, self.NHKV = H, I, NHQ, NHKV
        self.B, self.S, self.n_cores = B, S, n_cores
        self.theta, self.eps = theta, eps
        self.D = 128
        assert NHQ * self.D == H
        self.KVD = NHKV * self.D
        self.HT = H // 128
        self.IT = I // 128
        self.KVT = self.KVD // 128
        self.CH = 512
        self.CT = self.CH // 128
        self.SC = S // self.CH
        assert n_cores == 2 * B
        self.NC = self.SC // 2
        self.GRP = NHQ // NHKV
        self.posX = [p for p in range(self.SC) if p % 4 in (0, 3)]
        self.posY = [p for p in range(self.SC) if p % 4 in (1, 2)]
        assert len(self.posX) == self.NC and len(self.posY) == self.NC
        self.slot_blocks = [max(self.posX[s], self.posY[s]) + 1
                            for s in range(self.NC)]


class Buf:
    __slots__ = ("name", "writers", "readers", "view")

    def __init__(self, name, view=None):
        self.name = name
        self.writers = []
        self.readers = []
        self.view = view


class Sem:
    __slots__ = ("h", "n", "name")

    def __init__(self, h, name):
        self.h = h
        self.n = 0
        self.name = name


class DStream:

    def __init__(self, k, name, nslots=2):
        self.sems = [k.mk_sem(f"{name}{i}") for i in range(nslots)]


class K:
    ENGS = {"t": "tensor", "v": "vector", "a": "scalar", "s": "sync",
            "g": "gpsimd"}

    def __init__(self, nc, ex, blk):
        self.nc = nc
        self.ex = ex
        self.blk = blk
        self._semnames = 0
        self.esem = {e: self.mk_sem(f"e_{e}") for e in ("t", "v", "a", "g")}
        self.wm = {e: {} for e in self.ENGS}

    def mk_sem(self, name):
        h = self.ex.enter_context(self.nc.semaphore(name))
        return Sem(h, name)

    def _wait(self, eng, ename, tok):
        sem, val = tok
        if val <= 0:
            return
        if self.wm[ename].get(id(sem), -1) >= val:
            return
        eng.wait_ge(sem.h, val)
        self.wm[ename][id(sem)] = val

    def wait_reads(self, eng, ename, bufs):
        for b in bufs:
            for tok in b.writers:
                self._wait(eng, ename, tok)

    def wait_writes(self, eng, ename, bufs):
        for b in bufs:
            for tok in b.writers:
                self._wait(eng, ename, tok)
            for tok in b.readers:
                self._wait(eng, ename, tok)

    def note(self, ename, tok, reads=(), writes=()):
        for b in reads:
            b.readers.append(tok)
        for b in writes:
            b.writers = [tok]
            b.readers = []

    def on(self, ename, body, reads=(), writes=()):
        sem = self.esem[ename]
        out = {}

        def _f(eng):
            self.wait_reads(eng, ename, reads)
            self.wait_writes(eng, ename, writes)
            last = body(eng)
            assert last is not None, "step body must return last instruction"
            last.then_inc(sem.h, 1)
            sem.n += 1
            out["tok"] = (sem, sem.n)

        getattr(self.blk, self.ENGS[ename])(_f)
        tok = out["tok"]
        self.note(ename, tok, reads, writes)
        return tok

    def dma(self, qname, sem, transfers, reads=(), writes=()):
        out = {}

        def _f(eng):
            self.wait_reads(eng, qname, reads)
            self.wait_writes(eng, qname, writes)
            for tr in transfers:
                dst, src = tr[0], tr[1]
                kw = tr[2] if len(tr) > 2 else {}
                eng.dma_start(out=dst, in_=src, **kw).then_inc(sem.h, 16)
                sem.n += 16
            out["n"] = sem.n

        getattr(self.blk, self.ENGS[qname])(_f)
        tok = (sem, out["n"])
        self.note(qname, tok, reads, writes)
        return tok

    def raw(self, ename, body):
        getattr(self.blk, self.ENGS[ename])(lambda eng: body(eng))

    def batch(self, ename, items):
        sem = self.esem[ename]
        toks = []

        def _f(eng):
            for body, reads, writes in items:
                self.wait_reads(eng, ename, reads)
                self.wait_writes(eng, ename, writes)
                last = body(eng)
                last.then_inc(sem.h, 1)
                sem.n += 1
                tok = (sem, sem.n)
                self.note(ename, tok, reads, writes)
                toks.append(tok)

        getattr(self.blk, self.ENGS[ename])(_f)
        return toks

    def barrier(self, dma_sems=()):
        toks = [(s, s.n) for s in self.esem.values()]
        toks += [(s, s.n) for s in dma_sems]
        for ename in ("t", "v", "a", "s", "g"):
            def _f(eng, ename=ename):
                for tok in toks:
                    self._wait(eng, ename, tok)
            getattr(self.blk, self.ENGS[ename])(_f)


def ceil_div(a, b):
    return -(-a // b)


def _cast_dma(k, sem, dst_ap, src_ap, rows, row_split=2048):
    trs = []
    for r0 in range(0, rows, row_split):
        r1 = min(rows, r0 + row_split)
        trs.append((dst_ap(r0, r1), src_ap(r0, r1)))
    return k.dma("g", sem, trs)


def build_program(cfg: Cfg):
    c = cfg
    nc = bass.Bass("TRN2", target_bir_lowering=False)

    dr = lambda n, sh, dt, kind: nc.dram_tensor(n, sh, dt, kind=kind).ap()
    sb = lambda n, sh, dt: nc.sbuf_tensor(n, sh, dt,
                                          allow_name_mangling=True)
    xfull = dr("xfull", [c.S, c.H], F32, "ExternalInput")
    xq = dr("xq", [c.NC * c.CH, c.H], F32, "ExternalInput")
    xqT = dr("xqT", [c.H, c.NC * c.CH], F32, "ExternalInput")
    wq = dr("wq", [c.H, c.H], F32, "ExternalInput")
    wk = dr("wk", [c.H, c.KVD], F32, "ExternalInput")
    wv = dr("wv", [c.H, c.KVD], F32, "ExternalInput")
    wo = dr("wo", [c.H, c.H], F32, "ExternalInput")
    gwT = dr("gwT", [c.H, c.I], I32, "ExternalInput")
    uwT = dr("uwT", [c.H, c.I], I32, "ExternalInput")
    dwT = dr("dwT", [c.I, c.H], I32, "ExternalInput")
    gs = dr("gs", [128, c.IT], F32, "ExternalInput")
    us = dr("us", [128, c.IT], F32, "ExternalInput")
    ds = dr("ds", [128, c.HT], F32, "ExternalInput")
    cosk = dr("cosk", [64, c.S], F32, "ExternalInput")
    sink = dr("sink", [64, c.S], F32, "ExternalInput")
    nsink = dr("nsink", [64, c.S], F32, "ExternalInput")
    cosq = dr("cosq", [64, c.NC * c.CH], F32, "ExternalInput")
    sinq = dr("sinq", [64, c.NC * c.CH], F32, "ExternalInput")
    nsinq = dr("nsinq", [64, c.NC * c.CH], F32, "ExternalInput")
    maskT = dr("maskT", [c.NC * 8 * 128, 512], F32, "ExternalInput")
    ident = dr("ident", [128, 128], BF16, "ExternalInput")
    outT = dr("outT", [c.H, c.NC * c.CH], F32, "ExternalOutput")

    wq_c = dr("wq_c", [c.HT, 128, c.HT * 128], BF16, "Internal")
    wo_c = dr("wo_c", [c.HT, 128, c.HT * 128], BF16, "Internal")
    wk_c = dr("wk_c", [128, c.HT * c.KVD], BF16, "Internal")
    wv_c = dr("wv_c", [128, c.HT * c.KVD], BF16, "Internal")
    g_c = dr("g_c", [c.IT, 128, c.HT * 128], BF16, "Internal")
    u_c = dr("u_c", [c.IT, 128, c.HT * 128], BF16, "Internal")
    d_c = dr("d_c", [c.HT, 128, c.IT * 128], BF16, "Internal")
    xnT_h = dr("xnT_h", [c.SC, 128, c.HT * c.CH], BF16, "Internal")
    kt_h = dr("kt_h", [c.NHKV, 128, c.S], BF16, "Internal")
    v_h = dr("v_h", [c.NHKV, c.S, 128], BF16, "Internal")
    ffn_h = dr("ffn_h", [c.NC, c.HT, 128, c.CH], F32, "Internal")

    with contextlib.ExitStack() as EX:
        e = EX.enter_context
        blk = e(nc.Block())
        k = K(nc, EX, blk)

        B_wq_c = Buf("wq_c"); B_wo_c = Buf("wo_c")
        B_wk_c = Buf("wk_c"); B_wv_c = Buf("wv_c")
        B_g_c = Buf("g_c"); B_u_c = Buf("u_c"); B_d_c = Buf("d_c")
        B_xnT = [Buf(f"xnT{i}") for i in range(c.SC)]
        B_kt = Buf("kt_h"); B_v = Buf("v_h")
        B_ffn = [Buf(f"ffn{i}") for i in range(c.NC)]
        B_out = Buf("outT")

        s_cast = {n: k.mk_sem(f"cast_{n}") for n in
                  ("kv", "gu", "dn", "wq", "wo")}
        st_x = DStream(k, "sx", 4)
        st_w = DStream(k, "sw", 3)
        st_f = DStream(k, "sf")
        st_kv = DStream(k, "skv")
        st_sp = DStream(k, "ssp")
        st_w2 = DStream(k, "sw2", 3)
        st_out = DStream(k, "sot")
        s_misc = k.mk_sem("smisc")

        _cast_dma(
            k, s_cast["kv"],
            lambda r0, r1: wk_c[:, r0 // 128 * c.KVD:(r1 // 128) * c.KVD]
            .rearrange("p (t d) -> p t d", d=c.KVD),
            lambda r0, r1: wk[r0:r1, :].rearrange("(t p) d -> p t d", p=128),
            c.H)
        _cast_dma(
            k, s_cast["kv"],
            lambda r0, r1: wv_c[:, r0 // 128 * c.KVD:(r1 // 128) * c.KVD]
            .rearrange("p (t d) -> p t d", d=c.KVD),
            lambda r0, r1: wv[r0:r1, :].rearrange("(t p) d -> p t d", p=128),
            c.H)
        k.note("g", (s_cast["kv"], s_cast["kv"].n),
               writes=[B_wk_c, B_wv_c])

        def cache_col_slabs(sem, dst, src, m, H_rows):
            _cast_dma(
                k, sem,
                lambda r0, r1: dst[m, :, r0:r1]
                .rearrange("p (t o) -> p t o", o=128),
                lambda r0, r1: src[r0:r1, m * 128:(m + 1) * 128]
                .rearrange("(t p) o -> p t o", p=128),
                H_rows)

        for m in range(c.IT):
            cache_col_slabs(s_cast["gu"], g_c, gwT, m, c.H)
            cache_col_slabs(s_cast["gu"], u_c, uwT, m, c.H)
        k.note("g", (s_cast["gu"], s_cast["gu"].n), writes=[B_g_c, B_u_c])
        for m in range(c.HT):
            cache_col_slabs(s_cast["dn"], d_c, dwT, m, c.I)
        k.note("g", (s_cast["dn"], s_cast["dn"].n), writes=[B_d_c])
        for m in range(c.HT):
            cache_col_slabs(s_cast["wq"], wq_c, wq, m, c.H)
        k.note("g", (s_cast["wq"], s_cast["wq"].n), writes=[B_wq_c])
        for m in range(c.HT):
            cache_col_slabs(s_cast["wo"], wo_c, wo, m, c.H)
        k.note("g", (s_cast["wo"], s_cast["wo"].n), writes=[B_wo_c])

        ps_sc = e(nc.psum_tensor("ps_sc", [128, 3072], F32))
        ps_hi = [e(nc.psum_tensor(f"ps{i}", [128, 512], F32))
                 for i in range(6, 8)]
        ps = [ps_sc[:, j * 512:(j + 1) * 512] for j in range(6)] + \
            [p[:] for p in ps_hi]
        B_ps = [Buf(f"ps{i}") for i in range(8)]

        id_sb = e(sb("id_sb", [128, 128], BF16))
        B_id = Buf("id_sb")
        eps_sb = e(sb("eps_sb", [128, 1], F32))
        nhalf_sb = e(sb("nhalf_sb", [128, 1], F32))
        B_eps = Buf("eps"); B_nhalf = Buf("nhalf")
        k.dma("s", s_misc, [(id_sb[:], ident)], writes=[B_id])
        k.on("v", lambda v: (v.memset(eps_sb[:], c.eps),
                             v.memset(nhalf_sb[:], -0.5))[-1],
             writes=[B_eps, B_nhalf])

        def ln_tiles(x_src_rows, xnT_sb, B_xnT_sb, st, bslots, ntiles,
                     ps_lo, ldstream_bufs):
            def chain(i):
                p = i % len(ldstream_bufs)
                xb, B_xb = ldstream_bufs[p]
                k.dma("s", st.sems[p], [(xb[:], x_src_rows(i))],
                      writes=[B_xb])
                stats, B_stats = ln_stats[p % 2]
                mvb, B_mv = ln_mv[p % 2]
                rstd, B_rstd = ln_rstd[p % 2]
                nmr, B_nmr = ln_nmr[p % 2]
                xnb, B_xnb = ln_xn[i % 4]
                nst = ceil_div(c.H, 512)

                def bnst(v, xb=xb, nst=nst, stats=stats):
                    last = None
                    for ii in range(nst):
                        w = min(512, c.H - ii * 512)
                        last = v.bn_stats(
                            out=stats[:, ii * 6:(ii + 1) * 6],
                            in_=xb[:, ii * 512:ii * 512 + w])
                    return last
                k.on("v", bnst, reads=[B_xb], writes=[B_stats])
                k.on("v", lambda v: v.bn_aggr(out=mvb[:], in_=stats[:]),
                     reads=[B_stats], writes=[B_mv])
                k.batch("a", [
                    (lambda a: a.activation(
                        out=rstd[:], in_=mvb[:, 1:2], func=AF.Ln,
                        bias=eps_sb[:, 0:1], scale=1.0),
                     [B_mv, B_eps], [B_rstd]),
                    (lambda a: a.activation(
                        out=rstd[:], in_=rstd[:], func=AF.Exp, bias=0.0,
                        scale=nhalf_sb[:, 0:1]),
                     [B_nhalf], [B_rstd]),
                ])
                k.on("v", lambda v: v.tensor_scalar(
                    out=nmr[:], in0=mvb[:, 0:1], scalar1=rstd[:, 0:1],
                    scalar2=-1.0, op0=ALU.mult, op1=ALU.mult),
                    reads=[B_mv, B_rstd], writes=[B_nmr])
                k.on("v", lambda v, xb=xb: v.tensor_scalar(
                    out=xnb[:], in0=xb[:], scalar1=rstd[:, 0:1],
                    scalar2=nmr[:, 0:1], op0=ALU.mult, op1=ALU.add),
                    reads=[B_xb, B_nmr, B_rstd], writes=[B_xnb])

            def transposes(i):
                xnb, B_xnb = ln_xn[i % 4]
                for j0 in range(0, c.HT, 4):
                    bank = ps_lo + (j0 // 4) % 2
                    nj = min(4, c.HT - j0)

                    def mmb(t_, j0=j0, nj=nj, bank=bank, xnb=xnb):
                        last = None
                        for j in range(nj):
                            last = t_.matmul(
                                ps[bank][:, j * 128:(j + 1) * 128],
                                xnb[:, (j0 + j) * 128:(j0 + j + 1) * 128],
                                id_sb[:], start=True, stop=True)
                        return last
                    k.on("t", mmb, reads=[B_xnb, B_id], writes=[B_ps[bank]])

                    def evac(v, j0=j0, nj=nj, bank=bank, i=i):
                        return v.tensor_copy(
                            out=xnT_sb[:].rearrange(
                                "p (h q) -> p h q", q=bslots * 128)
                            [:, j0:j0 + nj, i * 128:(i + 1) * 128],
                            in_=ps[bank][:, 0:nj * 128].rearrange(
                                "p (h q) -> p h q", q=128))
                    k.on("v", evac, reads=[B_ps[bank]], writes=[B_xnT_sb])

            la = min(3, ntiles)
            for i in range(la):
                chain(i)
            for i in range(ntiles):
                if i + la < ntiles:
                    chain(i + la)
                transposes(i)

        with contextlib.ExitStack() as PA:
            pe = PA.enter_context
            xbufs = [(pe(sb(f"a1_x{j}", [128, c.H], F32)),
                      Buf(f"a1_x{j}")) for j in range(4)]
            ln_stats = [(pe(sb(f"ln_st{j}", [128, 6 * ceil_div(c.H, 512)],
                                F32)), Buf(f"lns{j}")) for j in range(2)]
            ln_mv = [(pe(sb(f"ln_mv{j}", [128, 2], F32)), Buf(f"lnm{j}"))
                     for j in range(2)]
            ln_rstd = [(pe(sb(f"ln_rs{j}", [128, 1], F32)), Buf(f"lnr{j}"))
                       for j in range(2)]
            ln_nmr = [(pe(sb(f"ln_nm{j}", [128, 1], F32)), Buf(f"lnn{j}"))
                      for j in range(2)]
            ln_xn = [(pe(sb(f"ln_xn{j}", [128, c.H], BF16)),
                      Buf(f"lnx{j}")) for j in range(4)]
            xt0 = pe(sb("a1_t0", [128, c.HT * c.CH], BF16))
            xt1 = pe(sb("a1_t1", [128, c.HT * c.CH], BF16))
            xts = [(xt0, Buf("a1_t0")), (xt1, Buf("a1_t1"))]

            for kc in range(c.SC):
                xnT_sb, B_xnT_sb = xts[kc % 2]

                def xsrc(i, kc=kc):
                    r = kc * c.CH + i * 128
                    return xfull[r:r + 128, :]
                ln_tiles(xsrc, xnT_sb, B_xnT_sb, st_x, c.CT, c.CT, 0, xbufs)
                k.dma("a", st_sp.sems[kc % 2],
                      [(xnT_h[kc], xnT_sb[:])],
                      reads=[B_xnT_sb], writes=[B_xnT[kc]])

            k.barrier(dma_sems=[st_x.sems[0], st_x.sems[1],
                                st_sp.sems[0], st_sp.sems[1], s_misc])

        with contextlib.ExitStack() as PA:
            pe = PA.enter_context
            wk_sb = pe(sb("wk_sb", [128, c.HT * c.KVD], BF16))
            B_wk_sb = Buf("wk_sb")
            ck = pe(sb("ck", [64, c.S], F32)); B_ck = Buf("ck")
            sk = pe(sb("sk", [64, c.S], F32)); B_sk = Buf("sk")
            nk = pe(sb("nk", [64, c.S], F32)); B_nk = Buf("nk")
            k.dma("s", s_misc, [(wk_sb[:], wk_c), (ck[:], cosk),
                                (sk[:], sink), (nk[:], nsink)],
                  reads=[B_wk_c],
                  writes=[B_wk_sb, B_ck, B_sk, B_nk])
            xt0 = pe(sb("a2_t0", [128, c.HT * c.CH], BF16))
            xt1 = pe(sb("a2_t1", [128, c.HT * c.CH], BF16))
            xts = [(xt0, Buf("a2_t0")), (xt1, Buf("a2_t1"))]
            tmpA = pe(sb("rp_a", [128, c.CH], F32))
            tmpB = pe(sb("rp_b", [128, c.CH], F32))
            B_tA, B_tB = Buf("rp_a"), Buf("rp_b")
            kst0 = pe(sb("kst0", [128, c.CH], BF16))
            kst1 = pe(sb("kst1", [128, c.CH], BF16))
            ksts = [(kst0, Buf("kst0")), (kst1, Buf("kst1"))]

            for kc in range(c.SC):
                xnT_sb, B_xnT_sb = xts[kc % 2]
                k.dma("s", st_x.sems[kc % 2], [(xnT_sb[:], xnT_h[kc])],
                      reads=[B_xnT[kc]], writes=[B_xnT_sb])
                for m in range(c.KVT):
                    bank = m % 2

                    def mmk(t_, m=m, xnT_sb=xnT_sb):
                        last = None
                        for t in range(c.HT):
                            last = t_.matmul(
                                ps[bank],
                                wk_sb[:, t * c.KVD + m * 128:
                                      t * c.KVD + (m + 1) * 128],
                                xnT_sb[:, t * c.CH:(t + 1) * c.CH],
                                start=(t == 0), stop=(t == c.HT - 1))
                        return last
                    k.on("t", mmk, reads=[B_wk_sb, B_xnT_sb],
                         writes=[B_ps[bank]])
                    kcs = slice(kc * c.CH, (kc + 1) * c.CH)

                    def rope1(v, bank=bank, kcs=kcs):
                        v.tensor_mul(tmpA[0:64, :], ps[bank][0:64, :],
                                     ck[:, kcs])
                        v.tensor_mul(tmpA[64:128, :], ps[bank][64:128, :],
                                     ck[:, kcs])
                        v.tensor_mul(tmpB[0:64, :], ps[bank][64:128, :],
                                     nk[:, kcs])
                        return v.tensor_mul(tmpB[64:128, :], ps[bank][0:64, :],
                                            sk[:, kcs])
                    k.on("v", rope1, reads=[B_ps[bank], B_ck, B_sk, B_nk],
                         writes=[B_tA, B_tB])
                    kst, B_kst = ksts[m % 2]
                    k.on("v", lambda v: v.tensor_add(kst[:], tmpA[:], tmpB[:]),
                         reads=[B_tA, B_tB], writes=[B_kst])
                    k.dma("a", st_sp.sems[m % 2],
                          [(kt_h[m][:, kc * c.CH:(kc + 1) * c.CH], kst[:])],
                          reads=[B_kst], writes=[B_kt])
            k.barrier(dma_sems=[st_x.sems[0], st_x.sems[1],
                                st_sp.sems[0], st_sp.sems[1], s_misc])

        with contextlib.ExitStack() as PA:
            pe = PA.enter_context
            wv_sb = pe(sb("wv_sb", [128, c.HT * c.KVD], BF16))
            B_wv_sb = Buf("wv_sb")
            k.dma("s", s_misc, [(wv_sb[:], wv_c)], reads=[B_wv_c],
                  writes=[B_wv_sb])
            xt0 = pe(sb("a3_t0", [128, c.HT * c.CH], BF16))
            xt1 = pe(sb("a3_t1", [128, c.HT * c.CH], BF16))
            xts = [(xt0, Buf("a3_t0")), (xt1, Buf("a3_t1"))]
            vst0 = pe(sb("vst0", [128, c.KVD], BF16))
            vst1 = pe(sb("vst1", [128, c.KVD], BF16))
            vsts = [(vst0, Buf("vst0")), (vst1, Buf("vst1"))]

            for kc in range(c.SC):
                xnT_sb, B_xnT_sb = xts[kc % 2]
                k.dma("s", st_x.sems[kc % 2], [(xnT_sb[:], xnT_h[kc])],
                      reads=[B_xnT[kc]], writes=[B_xnT_sb])
                for t4 in range(c.CT):
                    vst, B_vst = vsts[t4 % 2]
                    nhalf = ceil_div(c.KVD, 512)
                    for n in range(nhalf):
                        bank = n % 2
                        nw = min(512, c.KVD - n * 512)

                        def mmv(t_, t4=t4, n=n, nw=nw, bank=bank,
                                xnT_sb=xnT_sb):
                            last = None
                            for t in range(c.HT):
                                last = t_.matmul(
                                    ps[bank][:, 0:nw],
                                    xnT_sb[:, t * c.CH + t4 * 128:
                                           t * c.CH + (t4 + 1) * 128],
                                    wv_sb[:, t * c.KVD + n * 512:
                                          t * c.KVD + n * 512 + nw],
                                    start=(t == 0), stop=(t == c.HT - 1))
                            return last
                        k.on("t", mmv, reads=[B_wv_sb, B_xnT_sb],
                             writes=[B_ps[bank]])
                        k.on("v", lambda v, n=n, nw=nw, bank=bank:
                             v.tensor_copy(out=vst[:, n * 512:n * 512 + nw],
                                           in_=ps[bank][:, 0:nw]),
                             reads=[B_ps[bank]], writes=[B_vst])
                    row = kc * c.CH + t4 * 128
                    k.dma("a", st_sp.sems[t4 % 2],
                          [(v_h[g][row:row + 128, :],
                            vst[:, g * 128:(g + 1) * 128])
                           for g in range(c.NHKV)],
                          reads=[B_vst], writes=[B_v])
            k.barrier(dma_sems=[st_x.sems[0], st_x.sems[1],
                                st_sp.sems[0], st_sp.sems[1], s_misc])

        gs_sb = e(sb("gs_sb", [128, c.IT], F32))
        us_sb = e(sb("us_sb", [128, c.IT], F32))
        ds_sb = e(sb("ds_sb", [128, c.HT], F32))
        B_gs, B_us, B_ds = Buf("gs"), Buf("us"), Buf("ds")
        k.dma("s", s_misc,
              [(gs_sb[:], gs), (us_sb[:], us), (ds_sb[:], ds)],
              writes=[B_gs, B_us, B_ds])

        for ci in range(c.NC):
            bs = c.slot_blocks[ci]
            nkk = bs * (512 // 128)
            with contextlib.ExitStack() as PC:
                pe = PC.enter_context
                QT = pe(nc.sbuf_tensor("b_QT", [128, c.NHQ * c.CH], BF16,
                                       side="right",
                                       allow_name_mangling=True))
                B_QT = Buf("b_QT")
                xnT_cm = nc.sbuf_tensor("b_xnT", [128, c.HT * c.CH], BF16,
                                        side="right",
                                        allow_name_mangling=True)
                xnT_sb = xnT_cm.__enter__()
                B_xnT_sb = Buf("b_xnT")

                with contextlib.ExitStack() as P0:
                    p0 = P0.enter_context
                    xbufs = [(p0(sb(f"b0_x{j}", [128, c.H], F32)),
                              Buf(f"b0_x{j}")) for j in range(4)]
                    ln_stats = [(p0(sb(f"b0_st{j}",
                                       [128, 6 * ceil_div(c.H, 512)], F32)),
                                 Buf(f"b0s{j}")) for j in range(2)]
                    ln_mv = [(p0(sb(f"b0_mv{j}", [128, 2], F32)),
                              Buf(f"b0m{j}")) for j in range(2)]
                    ln_rstd = [(p0(sb(f"b0_rs{j}", [128, 1], F32)),
                                Buf(f"b0r{j}")) for j in range(2)]
                    ln_nmr = [(p0(sb(f"b0_nm{j}", [128, 1], F32)),
                               Buf(f"b0n{j}")) for j in range(2)]
                    ln_xn = [(p0(sb(f"b0_xn{j}", [128, c.H], BF16)),
                              Buf(f"b0x{j}")) for j in range(4)]

                    def xsrc(i, ci=ci):
                        r = ci * c.CH + i * 128
                        return xq[r:r + 128, :]
                    ln_tiles(xsrc, xnT_sb, B_xnT_sb, st_x, c.CT, c.CT, 0,
                             xbufs)
                    k.barrier(dma_sems=[st_x.sems[0], st_x.sems[1]])

                with contextlib.ExitStack() as P1:
                    p1 = P1.enter_context
                    interT = p1(sb("interT",
                                               [128, c.IT * c.CH], BF16))
                    B_inter = Buf("interT")
                    P1a = p1(contextlib.ExitStack())
                    p1a = P1a.enter_context
                    gsl0 = p1a(sb("b1_g0", [128, c.HT * 128], BF16))
                    gsl1 = p1a(sb("b1_g1", [128, c.HT * 128], BF16))
                    usl0 = p1a(sb("b1_u0", [128, c.HT * 128], BF16))
                    usl1 = p1a(sb("b1_u1", [128, c.HT * 128], BF16))
                    gslabs = [(gsl0, Buf("b1_g0")), (gsl1, Buf("b1_g1"))]
                    uslabs = [(usl0, Buf("b1_u0")), (usl1, Buf("b1_u1"))]
                    gsts = [(p1a(sb(f"b1_g{j}", [128, c.CH], BF16)),
                             Buf(f"b1_g{j}")) for j in range(4)]

                    def mm(t_, slab, bank):
                        last = None
                        for t in range(c.HT):
                            last = t_.matmul(
                                ps[bank],
                                slab[:, t * 128:(t + 1) * 128],
                                xnT_sb[:, t * c.CH:(t + 1) * c.CH],
                                start=(t == 0), stop=(t == c.HT - 1))
                        return last

                    for m0 in range(0, c.IT, 4):
                        ng = min(4, c.IT - m0)
                        for m in range(m0, m0 + ng):
                            j = m - m0
                            p = m % 2
                            gslab, B_gslab = gslabs[p]
                            uslab, B_uslab = uslabs[p]
                            k.dma("s", st_w.sems[p], [(gslab[:], g_c[m])],
                                  reads=[B_g_c], writes=[B_gslab])
                            k.dma("s", st_f.sems[p], [(uslab[:], u_c[m])],
                                  reads=[B_u_c], writes=[B_uslab])
                            k.on("t", lambda t_, s=gslab, b=j: mm(t_, s, b),
                                 reads=[B_gslab, B_xnT_sb],
                                 writes=[B_ps[j]])
                            k.on("t", lambda t_, s=uslab, b=4 + j:
                                 mm(t_, s, b),
                                 reads=[B_uslab, B_xnT_sb],
                                 writes=[B_ps[4 + j]])
                        items = []
                        for m in range(m0, m0 + ng):
                            j = m - m0
                            gstb, B_gstb = gsts[j]

                            def dosilu(a, b=j, gstb=gstb, m=m):
                                return a.activation(
                                    out=gstb[:], in_=ps[b],
                                    func=AF.Silu, bias=0.0,
                                    scale=gs_sb[:, m:m + 1])
                            items.append((dosilu, [B_ps[j], B_gs],
                                          [B_gstb]))
                        k.batch("a", items)
                        items = []
                        for m in range(m0, m0 + ng):
                            j = m - m0
                            gstb, B_gstb = gsts[j]

                            def dostt(v, b=4 + j, gstb=gstb, m=m):
                                return v.scalar_tensor_tensor(
                                    out=interT[:, m * c.CH:(m + 1) * c.CH],
                                    in0=ps[b],
                                    scalar=us_sb[:, m:m + 1],
                                    in1=gstb[:], op0=ALU.mult, op1=ALU.mult)
                            items.append((dostt, [B_ps[4 + j], B_us, B_gstb],
                                          [B_inter]))
                        k.batch("v", items)
                    k.barrier(dma_sems=[st_w.sems[0], st_w.sems[1],
                                        st_f.sems[0], st_f.sems[1]])
                    P1a.close()

                    with contextlib.ExitStack() as P3:
                        p3 = P3.enter_context
                        sl0 = p3(sb("b3_s0", [128, c.HT * 128], BF16))
                        sl1 = p3(sb("b3_s1", [128, c.HT * 128], BF16))
                        slabs = [(sl0, Buf("b3_s0")), (sl1, Buf("b3_s1"))]
                        cqb = p3(sb("cq", [64, c.CH], F32))
                        sqb = p3(sb("sq", [64, c.CH], F32))
                        nqb = p3(sb("nq", [64, c.CH], F32))
                        B_cq, B_sq, B_nq = Buf("cq"), Buf("sq"), Buf("nq")
                        qsl = slice(ci * c.CH, (ci + 1) * c.CH)
                        k.dma("s", s_misc, [(cqb[:], cosq[:, qsl]),
                                            (sqb[:], sinq[:, qsl]),
                                            (nqb[:], nsinq[:, qsl])],
                              writes=[B_cq, B_sq, B_nq])
                        tmpA = p3(sb("b3_a", [128, c.CH], F32))
                        tmpB = p3(sb("b3_b", [128, c.CH], F32))
                        B_tA, B_tB = Buf("b3_a"), Buf("b3_b")

                        for m in range(c.NHQ):
                            p = m % 2
                            slab, B_slab = slabs[p]
                            k.dma("s", st_w.sems[p], [(slab[:], wq_c[m])],
                                  reads=[B_wq_c], writes=[B_slab])
                            bank = m % 2

                            def mm(t_, slab=slab, bank=bank):
                                last = None
                                for t in range(c.HT):
                                    last = t_.matmul(
                                        ps[bank],
                                        slab[:, t * 128:(t + 1) * 128],
                                        xnT_sb[:, t * c.CH:(t + 1) * c.CH],
                                        start=(t == 0), stop=(t == c.HT - 1))
                                return last
                            k.on("t", mm, reads=[B_slab, B_xnT_sb],
                                 writes=[B_ps[bank]])

                            def rope1(v, bank=bank):
                                v.tensor_mul(tmpA[0:64, :], ps[bank][0:64, :],
                                             cqb[:])
                                v.tensor_mul(tmpA[64:128, :],
                                             ps[bank][64:128, :], cqb[:])
                                v.tensor_mul(tmpB[0:64, :],
                                             ps[bank][64:128, :], nqb[:])
                                return v.tensor_mul(tmpB[64:128, :],
                                                    ps[bank][0:64, :],
                                                    sqb[:])
                            k.on("v", rope1,
                                 reads=[B_ps[bank], B_cq, B_sq, B_nq],
                                 writes=[B_tA, B_tB])
                            k.on("v", lambda v, m=m: v.tensor_add(
                                QT[:, m * c.CH:(m + 1) * c.CH], tmpA[:], tmpB[:]),
                                reads=[B_tA, B_tB], writes=[B_QT])
                        k.barrier(dma_sems=[st_w.sems[0], st_w.sems[1], s_misc])

                    xnT_cm.__exit__(None, None, None)

                    with contextlib.ExitStack() as P2:
                        p2 = P2.enter_context
                        dslabs = [(p2(sb(f"b2_s{j}", [128, c.IT * 128],
                                         BF16)), Buf(f"b2_s{j}"))
                                  for j in range(3)]
                        dslab2s = [Buf(f"b2_s{j}h") for j in range(3)]
                        fst0 = p2(sb("b2_f0", [128, c.CH], F32))
                        fst1 = p2(sb("b2_f1", [128, c.CH], F32))
                        fsts = [(fst0, Buf("b2_f0")), (fst1, Buf("b2_f1"))]
                        for m in range(c.HT):
                            p = m % 3
                            slab, B_slab = dslabs[p]
                            h1 = c.IT // 2 * 128
                            B_slab2 = dslab2s[p]
                            k.dma("s", st_w.sems[p],
                                  [(slab[:, 0:h1], d_c[m, :, 0:h1])],
                                  reads=[B_d_c], writes=[B_slab])
                            k.dma("g", st_w2.sems[p],
                                  [(slab[:, h1:], d_c[m, :, h1:])],
                                  reads=[B_d_c], writes=[B_slab2])
                            bank = m % 2 + 2
                            kh = c.IT // 2

                            def mm1(t_, slab=slab, bank=bank, kh=kh):
                                last = None
                                for t in range(kh):
                                    last = t_.matmul(
                                        ps[bank],
                                        slab[:, t * 128:(t + 1) * 128],
                                        interT[:, t * c.CH:(t + 1) * c.CH],
                                        start=(t == 0), stop=False)
                                return last
                            k.on("t", mm1, reads=[B_slab, B_inter],
                                 writes=[B_ps[bank]])

                            def mm2(t_, slab=slab, bank=bank, kh=kh):
                                last = None
                                for t in range(kh, c.IT):
                                    last = t_.matmul(
                                        ps[bank],
                                        slab[:, t * 128:(t + 1) * 128],
                                        interT[:, t * c.CH:(t + 1) * c.CH],
                                        start=False, stop=(t == c.IT - 1))
                                return last
                            k.on("t", mm2, reads=[B_slab2, B_inter],
                                 writes=[B_ps[bank]])
                            fst, B_fst = fsts[m % 2]
                            k.on("v", lambda v, bank=bank, fst=fst, m=m:
                                 v.tensor_scalar(
                                     out=fst[:], in0=ps[bank],
                                     scalar1=ds_sb[:, m:m + 1], scalar2=None,
                                     op0=ALU.mult),
                                 reads=[B_ps[bank], B_ds], writes=[B_fst])
                            k.dma("a", st_sp.sems[m % 2],
                                  [(ffn_h[ci, m], fst[:])],
                                  reads=[B_fst], writes=[B_ffn[ci]])
                        k.barrier(dma_sems=[st_w.sems[0], st_w.sems[1],
                                            st_w2.sems[0], st_w2.sems[1],
                                            st_sp.sems[0], st_sp.sems[1]])

                aoT = pe(sb("b_aoT", [128, c.NHQ * c.CH], BF16))
                B_aoT = Buf("b_aoT")
                with contextlib.ExitStack() as P4:
                    p4 = P4.enter_context
                    mk_sb = p4(sb("mk_sb", [128, 8 * 512], BF16))
                    B_mk = Buf("mk_sb")
                    k.dma("g", s_cast["kv"],
                          [(mk_sb[:].rearrange("p (n w) -> p n w", w=512),
                            maskT[ci * 8 * 128:(ci + 1) * 8 * 128, :]
                            .rearrange("(n p) w -> p n w", p=128))],
                          writes=[B_mk])
                    kt0 = p4(sb("b4_k0", [128, bs * 512], BF16))
                    kt1 = p4(sb("b4_k1", [128, bs * 512], BF16))
                    kts = [(kt0, Buf("b4_k0")), (kt1, Buf("b4_k1"))]
                    vt0 = p4(sb("b4_v0", [128, nkk * 128], BF16))
                    vt1 = p4(sb("b4_v1", [128, nkk * 128], BF16))
                    vts = [(vt0, Buf("b4_v0")), (vt1, Buf("b4_v1"))]
                    natT = 3 if bs <= 6 else 2
                    atT2 = [(p4(sb(f"b4_aT{j}", [128, nkk * 512], BF16)),
                             Buf(f"b4_aT{j}")) for j in range(natT)]
                    b20 = p4(sb("b4_b20", [128, 1], F32))
                    B_b20 = Buf("b4_b20")
                    ones_col = p4(sb("b4_on", [128, 1], BF16))
                    B_ones = Buf("b4_on")
                    ones_row = p4(sb("b4_or", [1, 128], BF16))
                    B_onr = Buf("b4_or")
                    rcs = [dict(
                        rc_bf=p4(sb(f"b4_rc{j}", [1, 512], BF16)),
                        B_rcb=Buf(f"b4_rc{j}"),
                        rc_f=p4(sb(f"b4_rf{j}", [1, 512], F32)),
                        B_rcf=Buf(f"b4_rf{j}"),
                        sm_row=p4(sb(f"b4_sm{j}", [1, 512], F32)),
                        B_smr=Buf(f"b4_sm{j}")) for j in range(2)]
                    rcx = p4(sb("b4_rx", [128, 512], BF16))
                    B_rcx = Buf("b4_rx")

                    def minit(v):
                        v.memset(b20[:], -20.0)
                        v.memset(ones_col[:], 1.0)
                        return v.memset(ones_row[:], 1.0)
                    k.on("v", minit, writes=[B_b20, B_ones, B_onr])

                    def kv_load(g):
                        pkv = g % 2
                        ktb, B_ktb = kts[pkv]
                        vtb, B_vtb = vts[pkv]
                        k.dma("s", st_kv.sems[pkv],
                              [(ktb[:], kt_h[g][:, 0:bs * 512]),
                               (vtb[:].rearrange("p (t d) -> p t d", d=128),
                                v_h[g][0:nkk * 128, :].rearrange(
                                    "(t p) d -> p t d", p=128))],
                              reads=[B_kt, B_v], writes=[B_ktb, B_vtb])

                    ngrp = ceil_div(nkk, 3)
                    asem = k.esem["a"]
                    abase = asem.n
                    exp_jobs = []
                    nexp = [0]
                    parity_last = {0: None, 1: None}
                    last_exp_tok = [None]

                    def sc_exp(g, h, hi):
                        ktb, B_ktb = kts[g % 2]
                        atT, B_atT = atT2[hi % natT]
                        for gi in range(ngrp):
                            k0 = gi * 3
                            ng = min(3, nkk - k0)
                            b0 = (gi % 2) * 3
                            pt = parity_last[gi % 2]
                            wait_toks = []
                            for kk in range(k0, k0 + ng):
                                bank = b0 + (kk - k0)

                                def mmsc(t_, h=h, kk=kk, bank=bank,
                                         ktb=ktb, pt=pt):
                                    if pt is not None:
                                        k._wait(t_, "t", pt)
                                    return t_.matmul(
                                        ps[bank],
                                        ktb[:, kk * 128:(kk + 1) * 128],
                                        QT[:, h * c.CH:(h + 1) * c.CH],
                                        start=True, stop=True)
                                tok = k.on("t", mmsc, reads=[B_QT, B_ktb],
                                           writes=[B_ps[bank]])
                                wait_toks.append(tok)
                                if kk >= nkk - 8:
                                    j = kk - (nkk - 8)

                                    def msk(v, j=j, bank=bank, pt=pt):
                                        if pt is not None:
                                            k._wait(v, "v", pt)
                                        return v.tensor_add(
                                            ps[bank], ps[bank],
                                            mk_sb[:, j * 512:(j + 1) * 512])
                                    mtok = k.on("v", msk, reads=[B_mk],
                                                writes=[B_ps[bank]])
                                    wait_toks[-1] = mtok
                            if hi >= natT and gi == 0:
                                wait_toks = wait_toks + \
                                    list(atT2[hi % natT][1].readers)
                            wait_toks = wait_toks + list(B_b20.writers)
                            exp_jobs.append((wait_toks, k0, ng, b0, atT))
                            nexp[0] += 1
                            parity_last[gi % 2] = (asem, abase + nexp[0])
                            if gi == ngrp - 1:
                                last_exp_tok[0] = (asem, abase + nexp[0])

                    def flush_exps():
                        jobs = list(exp_jobs)
                        exp_jobs.clear()

                        def _f(a):
                            for wait_toks, k0, ng, b0, atT in jobs:
                                for tok in wait_toks:
                                    k._wait(a, "a", tok)
                                a.activation(
                                    out=atT[:, k0 * 512:(k0 + ng) * 512],
                                    in_=ps_sc[:, b0 * 512:(b0 + ng) * 512],
                                    func=AF.Exp, bias=b20[:, 0:1],
                                    scale=1.0).then_inc(asem.h, 1)
                                asem.n += 1
                        blk.scalar(_f)

                    lastexp_by_hi = {}

                    def sum_part(g, h, qh, hi):
                        atT, B_atT = atT2[hi % natT]
                        rr = rcs[hi % 2]
                        sm_row, B_smr = rr["sm_row"], rr["B_smr"]
                        rc_f, B_rcf = rr["rc_f"], rr["B_rcf"]
                        rc_bf, B_rcb = rr["rc_bf"], rr["B_rcb"]
                        lastexp = lastexp_by_hi[hi]

                        def mmsum(t_, atT=atT, lastexp=lastexp):
                            k._wait(t_, "t", lastexp)
                            last = None
                            for kk in range(nkk):
                                last = t_.matmul(
                                    ps[6][0:1, :], ones_col[:],
                                    atT[:, kk * 512:(kk + 1) * 512],
                                    start=(kk == 0), stop=(kk == nkk - 1))
                            return last
                        k.on("t", mmsum, reads=[B_ones], writes=[B_ps[6]])
                        k.on("v", lambda v: v.tensor_copy(
                            out=sm_row[:], in_=ps[6][0:1, :]),
                            reads=[B_ps[6]], writes=[B_smr])
                        k.on("v", lambda v: v.reciprocal(
                            out=rc_f[:], in_=sm_row[:]),
                            reads=[B_smr], writes=[B_rcf])
                        k.on("v", lambda v: v.tensor_copy(out=rc_bf[:],
                                                          in_=rc_f[:]),
                             reads=[B_rcf], writes=[B_rcb])

                    def av_part(g, h, qh, hi):
                        vtb, B_vtb = vts[g % 2]
                        atT, B_atT = atT2[hi % natT]
                        rr = rcs[hi % 2]
                        rc_bf, B_rcb = rr["rc_bf"], rr["B_rcb"]
                        k.on("t", lambda t_: t_.matmul(
                            ps[6], ones_row[:], rc_bf[:],
                            start=True, stop=True),
                            reads=[B_rcb, B_onr], writes=[B_ps[6]])
                        k.on("v", lambda v: v.tensor_copy(
                            out=rcx[:], in_=ps[6]),
                            reads=[B_ps[6]], writes=[B_rcx])

                        def mmav(t_, vtb=vtb, atT=atT):
                            last = None
                            for kk in range(nkk):
                                last = t_.matmul(
                                    ps[7],
                                    vtb[:, kk * 128:(kk + 1) * 128],
                                    atT[:, kk * 512:(kk + 1) * 512],
                                    start=(kk == 0), stop=(kk == nkk - 1))
                            return last
                        k.on("t", mmav, reads=[B_vtb, B_atT],
                             writes=[B_ps[7]])
                        k.on("v", lambda v, h=h: v.tensor_mul(
                            aoT[:, h * c.CH:(h + 1) * c.CH],
                            ps[7], rcx[:]),
                            reads=[B_ps[7], B_rcx], writes=[B_aoT])

                    heads = [(g, g * c.GRP + qh, qh)
                             for g in range(c.NHKV)
                             for qh in range(c.GRP)]
                    kv_load(0)
                    for hi, (g, h, qh) in enumerate(heads):
                        if qh == 0 and g + 1 < c.NHKV:
                            kv_load(g + 1)
                        sc_exp(g, h, hi)
                        lastexp_by_hi[hi] = last_exp_tok[0]
                        flush_exps()
                        if hi >= 1:
                            sum_part(*heads[hi - 1], hi - 1)
                        if natT >= 3:
                            if hi >= 2:
                                av_part(*heads[hi - 2], hi - 2)
                        else:
                            if hi >= 1:
                                av_part(*heads[hi - 1], hi - 1)
                    if natT >= 3:
                        sum_part(*heads[-1], len(heads) - 1)
                        av_part(*heads[-2], len(heads) - 2)
                        av_part(*heads[-1], len(heads) - 1)
                    else:
                        sum_part(*heads[-1], len(heads) - 1)
                        av_part(*heads[-1], len(heads) - 1)
                    assert asem.n == abase + nexp[0]
                    k.barrier(dma_sems=[st_kv.sems[0], st_kv.sems[1],
                                        s_misc])

                with contextlib.ExitStack() as P5:
                    p5 = P5.enter_context
                    slabs = [(p5(sb(f"b5_s{j}", [128, c.HT * 128], BF16)),
                              Buf(f"b5_s{j}")) for j in range(4)]
                    xqt_sb = p5(sb("b5_xqt",
                                               [128, c.HT * c.CH], F32))
                    B_xqt = Buf("b5_xqt")
                    k.dma("s", s_misc,
                          [(xqt_sb[:].rearrange("p (t q) -> p t q", q=c.CH),
                            xqT[:, ci * c.CH:(ci + 1) * c.CH].rearrange(
                                "(t p) q -> p t q", p=128))],
                          writes=[B_xqt])
                    fb0 = p5(sb("b5_f0", [128, c.CH], F32))
                    fb1 = p5(sb("b5_f1", [128, c.CH], F32))
                    fbs = [(fb0, Buf("b5_f0")), (fb1, Buf("b5_f1"))]
                    ob0 = p5(sb("b5_o0", [128, c.CH], F32))
                    ob1 = p5(sb("b5_o1", [128, c.CH], F32))
                    obs = [(ob0, Buf("b5_o0")), (ob1, Buf("b5_o1"))]
                    tb0 = p5(sb("b5_t0", [128, c.CH], F32))
                    tb1 = p5(sb("b5_t1", [128, c.CH], F32))
                    tbs = [(tb0, Buf("b5_t0")), (tb1, Buf("b5_t1"))]

                    for m in range(c.HT):
                        p4i = m % 4
                        slab, B_slab = slabs[p4i]
                        if p4i < 2:
                            k.dma("s", st_w.sems[p4i], [(slab[:], wo_c[m])],
                                  reads=[B_wo_c], writes=[B_slab])
                        else:
                            k.dma("g", st_w2.sems[p4i - 2],
                                  [(slab[:], wo_c[m])],
                                  reads=[B_wo_c], writes=[B_slab])
                        fb, B_fb = fbs[m % 2]
                        k.dma("s", st_f.sems[m % 2], [(fb[:], ffn_h[ci, m])],
                              reads=[B_ffn[ci]], writes=[B_fb])
                        bank = m % 2

                        def mm(t_, slab=slab, bank=bank):
                            last = None
                            for t in range(c.HT):
                                last = t_.matmul(
                                    ps[bank],
                                    slab[:, t * 128:(t + 1) * 128],
                                    aoT[:, t * c.CH:(t + 1) * c.CH],
                                    start=(t == 0), stop=(t == c.HT - 1))
                            return last
                        k.on("t", mm, reads=[B_slab, B_aoT],
                             writes=[B_ps[bank]])
                        tb, B_tb = tbs[m % 2]
                        ob, B_ob = obs[m % 2]
                        k.on("v", lambda v, bank=bank, tb=tb, fb=fb:
                             v.tensor_add(tb[:], ps[bank], fb[:]),
                             reads=[B_ps[bank], B_fb], writes=[B_tb])
                        k.on("v", lambda v, tb=tb, ob=ob, m=m: v.tensor_add(
                            ob[:], tb[:],
                            xqt_sb[:, m * c.CH:(m + 1) * c.CH]),
                            reads=[B_tb, B_xqt], writes=[B_ob])
                        k.dma("a", st_out.sems[m % 2],
                              [(outT[m * 128:(m + 1) * 128,
                                     ci * c.CH:(ci + 1) * c.CH], ob[:])],
                              reads=[B_ob], writes=[B_out])
                    k.barrier(dma_sems=[st_w.sems[0], st_w.sems[1],
                                        st_w2.sems[0], st_w2.sems[1],
                                        st_f.sems[0], st_f.sems[1], s_misc,
                                        st_out.sems[0], st_out.sems[1]])

        def _fin(eng):
            eng.wait_ge(st_out.sems[0].h, st_out.sems[0].n)
            eng.wait_ge(st_out.sems[1].h, st_out.sems[1].n)
        blk.sync(_fin)

    return nc


def _rope_tables(n, D, theta, scale):
    inv = 1.0 / theta ** (np.arange(0, D, 2, dtype=np.float64)[: D // 2] / D)
    freqs = np.arange(n, dtype=np.float64)[None, :] * inv[:, None]
    cos = (np.cos(freqs) * scale).astype(np.float32)
    sin = (np.sin(freqs) * scale).astype(np.float32)
    return cos, sin


def _mask_blocks(cfg, positions):
    c = cfg
    out = np.zeros((c.NC, 8, 128, 512), np.float32)
    for s in range(c.NC):
        bsl = c.slot_blocks[s]
        nkk = bsl * 4
        p = positions[s]
        qg = p * c.CH + np.arange(c.CH)[None, :]
        for j in range(8):
            kk = nkk - 8 + j
            keyg = kk * 128 + np.arange(128)[:, None]
            out[s, j] = np.where(keyg > qg, NEG, 0.0)
    return out


def make_in_maps(cfg, inputs):
    c = cfg
    x = np.ascontiguousarray(inputs["L0_attn_x"], np.float32)
    wq = np.ascontiguousarray(inputs["L0_attn_Wq"], np.float32)
    wk = np.ascontiguousarray(inputs["L0_attn_Wk"], np.float32)
    wv = np.ascontiguousarray(inputs["L0_attn_Wv"], np.float32)
    wo = np.ascontiguousarray(inputs["L0_attn_Wo"], np.float32)
    gwT = np.ascontiguousarray(inputs["L0_ffn_gate_wq"].T).astype(np.int32)
    uwT = np.ascontiguousarray(inputs["L0_ffn_up_wq"].T).astype(np.int32)
    dwT = np.ascontiguousarray(inputs["L0_ffn_down_wq"].T).astype(np.int32)
    gs = np.ascontiguousarray(
        np.asarray(inputs["L0_ffn_gate_scale"], np.float32)
        .reshape(c.IT, 128).T)
    us = np.ascontiguousarray(
        np.asarray(inputs["L0_ffn_up_scale"], np.float32)
        .reshape(c.IT, 128).T)
    ds = np.ascontiguousarray(
        np.asarray(inputs["L0_ffn_down_scale"], np.float32)
        .reshape(c.HT, 128).T)
    cosk, sink = _rope_tables(c.S, c.D, c.theta, 1.0)
    cosqf, sinqf = _rope_tables(c.S, c.D, c.theta, 1.0 / np.sqrt(c.D))
    ident = np.eye(128).astype(ml_dtypes.bfloat16)

    in_maps = []
    for core in range(c.n_cores):
        b = core // 2
        grp = core % 2
        positions = c.posX if grp == 0 else c.posY
        rows = np.concatenate(
            [np.arange(p * c.CH, (p + 1) * c.CH) for p in positions])
        xb = x[b]
        in_maps.append(dict(
            xfull=xb,
            xq=np.ascontiguousarray(xb[rows]),
            xqT=np.ascontiguousarray(xb[rows].T),
            wq=wq, wk=wk, wv=wv, wo=wo,
            gwT=gwT, uwT=uwT, dwT=dwT, gs=gs, us=us, ds=ds,
            cosk=cosk, sink=sink, nsink=-sink,
            cosq=np.ascontiguousarray(cosqf[:, rows]),
            sinq=np.ascontiguousarray(sinqf[:, rows]),
            nsinq=np.ascontiguousarray(-sinqf[:, rows]),
            maskT=_mask_blocks(c, positions).reshape(c.NC * 8 * 128, 512),
            ident=ident,
        ))
    return in_maps


def assemble(cfg, results):
    c = cfg
    out = np.empty((c.B, c.S, c.H), np.float32)
    for core, r in enumerate(results):
        b = core // 2
        grp = core % 2
        positions = c.posX if grp == 0 else c.posY
        oT = np.asarray(r["outT"])
        for s, p in enumerate(positions):
            out[b, p * c.CH:(p + 1) * c.CH, :] = \
                oT[:, s * c.CH:(s + 1) * c.CH].T
    return out


def kernel(**inputs):
    cfg = Cfg()
    nc = build_program(cfg)
    in_maps = make_in_maps(cfg, inputs)
    res = run_bass_kernel_spmd(nc, in_maps, core_ids=list(range(cfg.n_cores)))
    return assemble(cfg, res.results)
